# version 18
# baseline (speedup 1.0000x reference)
import numpy as np
import concourse.bass as bass
import concourse.mybir as mybir
from concourse.bass_utils import run_bass_kernel_spmd
from concourse.bass_isa import ReduceOp
from concourse import library_config
from concourse.library_overlay import lower_extended_insts
from contextlib import ExitStack

F32 = mybir.dt.float32
I32 = mybir.dt.int32
ALU = mybir.AluOpType
ACT = mybir.ActivationFunctionType
AX = mybir.AxisListType

D = 1024
DI = 2048
NS = 16
DTR = 64
HID = 3072
NKD = D // 128
NCT = DI // 128
NKH = HID // 128
EPS = 1e-5
QSCL = float(np.float32(1.0 / 127.0))

ENGINES = ("sync", "tensor", "vector", "scalar", "gpsimd")


class EM:
    def __init__(self):
        self.q = {e: [] for e in ENGINES}
        self.cnt = {}
        self.mark = {}

    def op(self, eng, fn, inc=None, deps=(), amt=1):
        flat = []
        for dp in deps:
            if dp is None:
                continue
            if isinstance(dp, (list, tuple)) and dp and isinstance(dp[0], (list, tuple)):
                flat.extend(x for x in dp if x is not None)
            else:
                flat.append(dp)
        for dp in flat:
            s, v = dp
            if self.mark.get((eng, s), 0) < v:
                self.q[eng].append(("w", s, v))
                self.mark[(eng, s)] = v
        ev = None
        if inc is not None:
            self.cnt[inc] = self.cnt.get(inc, 0) + amt
            ev = (inc, self.cnt[inc])
        self.q[eng].append(("o", fn, inc, amt))
        return ev

    def dma(self, eng, out, in_, sem, deps=()):
        prior = self.cnt.get(sem, 0)
        return self.op(eng, lambda e, o=out, i=in_: e.dma_start(out=o, in_=i),
                       inc=sem, deps=deps + ((sem, prior),), amt=16)

    def replay(self, block, sems):
        for name in ENGINES:
            section = getattr(block, name)

            def make(qname):
                def body(eng):
                    for item in self.q[qname]:
                        if item[0] == "w":
                            eng.wait_ge(sems[item[1]], item[2])
                        else:
                            inst = item[1](eng)
                            if item[2] is not None:
                                inst.then_inc(sems[item[2]], item[3])
                return body

            section(make(name))


def build_nc(L=4096, taps=False):
    TC = 256
    NCH = L // TC
    TS = TC
    TC9 = 128

    nc = bass.Bass("TRN2", target_bir_lowering=False, debug=False)

    dram = lambda n, s, dt=F32, k="ExternalInput": nc.dram_tensor(n, s, dt, kind=k).ap()
    xT = dram("xT", [D, L])
    inwT = {d: dram(f"inwT_{d}", [D, 2 * DI]) for d in "fb"}
    outw = {d: dram(f"outw_{d}", [D, DI]) for d in "fb"}
    cwR = {d: dram(f"cw_{d}", [128, NCT * 4]) for d in "fb"}
    cbR = {d: dram(f"cb_{d}", [128, NCT]) for d in "fb"}
    xpwR = {d: dram(f"xpw_{d}", [128, NCT * 96]) for d in "fb"}
    dtpwR = {d: dram(f"dtpw_{d}", [DTR, DI]) for d in "fb"}
    dtpbR = {d: dram(f"dtpb_{d}", [128, NCT]) for d in "fb"}
    AR = {d: dram(f"A_{d}", [128, NCT * NS]) for d in "fb"}
    DskR = {d: dram(f"Dsk_{d}", [128, NCT]) for d in "fb"}
    gamR = dram("gamR", [128, NKD])
    betR = dram("betR", [128, NKD])
    projTD = dram("projT", [2 * D, D])
    projbR = dram("projbR", [128, NKD])
    wgT = dram("wgT", [D, HID], I32)
    wuT = dram("wuT", [D, HID], I32)
    wdT = dram("wdT", [HID, D], I32)

    ks = "ExternalOutput" if taps else "Internal"
    WeT = {d: dram(f"WeT_{d}", [DI, D], F32, ks) for d in "fb"}
    xn1S = dram("xn1S", [D, L], F32, ks)
    xn2S = dram("xn2S", [D, L], F32, ks)
    uS = {d: dram(f"uS_{d}", [DI, L], F32, ks) for d in "fb"}
    zaS = {d: dram(f"zaS_{d}", [DI, L], F32, ks) for d in "fb"}
    dlS = {d: dram(f"dlS_{d}", [DI, L], F32, ks) for d in "fb"}
    bcSD = {d: dram(f"bcS_{d}", [32, L], F32, ks) for d in "fb"}
    ypS = {d: dram(f"ypS_{d}", [DI, L], F32, ks) for d in "fb"}
    yT = dram("yT", [D, L], F32, "ExternalOutput")

    def dview(t, p=128):
        return t.rearrange("(a p) t -> p a t", p=p)
    xTv = dview(xT)
    xn1v = dview(xn1S)
    xn2v = dview(xn2S)
    yTv = dview(yT)
    uV = {d: dview(uS[d]) for d in "fb"}
    zaV = {d: dview(zaS[d]) for d in "fb"}
    dlV = {d: dview(dlS[d]) for d in "fb"}
    ypV = {d: dview(ypS[d]) for d in "fb"}
    WeV = {d: dview(WeT[d]) for d in "fb"}

    em = EM()
    st = {}
    sev = st.__setitem__
    gev = st.get

    stack = ExitStack()
    sb = lambda name, shape, dt=F32: stack.enter_context(nc.sbuf_tensor(name, shape, dt))
    psmk = lambda name, shape: stack.enter_context(nc.psum_tensor(name, shape, F32))

    with stack:
        cw = {d: sb(f"cw{d}", [128, NCT * 4]) for d in "fb"}
        cb = {d: sb(f"cb{d}", [128, NCT]) for d in "fb"}
        dtpb = {d: sb(f"dtpb{d}", [128, NCT]) for d in "fb"}
        As = {d: sb(f"As{d}", [128, NCT * NS]) for d in "fb"}
        Dsk = {d: sb(f"Dsk{d}", [128, NCT]) for d in "fb"}
        projb = sb("projb", [128, NKD])
        gam = sb("gam", [128, NKD])
        bet = sb("bet", [128, NKD])
        hst = {d: sb(f"hst{d}", [128, NCT * NS]) for d in "fb"}
        halo = {d: sb(f"halo{d}", [128, NCT * 4]) for d in "fb"}

        p1x = sb("p1x", [128, NKD * TC])
        sqb = sb("sqb", [128, 2 * TC])
        bcb = sb("bcb", [128, 2 * TC])

        ipslab = sb("ipslab", [128, 2 * 1024])
        ipxn2 = sb("ipxn2", [128, NKD * TC])
        ipring = sb("ipring", [128, 2 * (TC + 8)])
        ipcvt = sb("ipcvt", [128, TC])
        ipout = sb("ipout", [128, 2 * TC])

        p4u = sb("p4u", [128, 2 * TC])
        p4dt = sb("p4dt", [64, TC])
        p4z0 = sb("p4z0", [128, 2 * TC])
        p4sp = sb("p4sp", [128, TC])
        p4xpw = sb("p4xpw", [128, 2 * 96])
        p4bc = sb("p4bc", [128, TC])
        p4dw = sb("p4dw", [64, 2 * 128])

        scdl = sb("scdl", [128, 2 * TS])
        scu = sb("scu", [128, 2 * TS])
        scza = sb("scza", [128, 2 * TS])
        scdA = sb("scdA", [128, NS * TS])
        scbh = sb("scbh", [128, NS * TS])
        scB = sb("scB", [128, NS * TS])
        scC = sb("scC", [128, NS * TS])
        scdu = sb("scdu", [128, TS])
        scy = sb("scy", [128, TS])
        scyp = sb("scyp", [128, TS])

        e_yf = sb("e_yf", [128, 2 * TC])
        e_yb = sb("e_yb", [128, 2 * TC])
        e_we = sb("e_we", [128, 2 * 1024])
        e_xn1 = sb("e_xn1", [128, 2 * TC])
        e_at = sb("e_at", [128, NKD * TC])
        e_x1 = sb("e_x1", [128, NKD * TC])
        e_x3 = sb("e_x3", [128, NKD * TC])
        e_hp = sb("e_hp", [128, NKH * TC])
        e_gt = sb("e_gt", [128, TC])
        e_wg = sb("e_wg", [128, 2 * 1024])
        e_wu = sb("e_wu", [128, 2 * 1024])
        e_wd = sb("e_wd", [128, 2 * 512])

        ps = [psmk(f"ps{i}", [128, 512]) for i in range(8)]

        blk_holder = []

        def prj_slice(kt, n2):
            buf = (ipxn2, p1x, e_at, e_x1)[kt // 2]
            off = (kt % 2) * 1024 + n2 * 512
            return buf[:, off:off + 512]

        def vop(fn, deps=(), inc="vv"):
            return em.op("vector", fn, inc=inc, deps=deps)

        def sop(fn, deps=(), inc="sv"):
            return em.op("scalar", fn, inc=inc, deps=deps)

        bank_free = {i: None for i in range(8)}

        def mmg(lst, banks, deps=(), inc="tv"):
            bdeps = tuple(bank_free[b] for b in banks)
            ev = None
            n = len(lst)
            for i, (o, l, r) in enumerate(lst):
                ev = em.op("tensor",
                           (lambda e, oo=o, ll=l, rr=r, s0=(i == 0), s1=(i == n - 1):
                            e.matmul(oo, ll, rr, start=s0, stop=s1)),
                           inc=(inc if i == n - 1 else None),
                           deps=(deps + bdeps) if i == 0 else ())
            return ev

        def bank_used(b, ev):
            bank_free[b] = ev

        def gop(fn, deps=(), inc="gv"):
            return em.op("gpsimd", fn, inc=inc, deps=deps)

        def stats(xbuf, T, deps, scl=1.0):
            assert T == TC
            dep_bcb = (gev("bcb_free"),)
            rsg = bcb[:, 0:T]
            xsm = bcb[:, TC:TC + T]
            x3 = xbuf[:].rearrange("p (a t) -> p t a", a=NKD)
            e_xs = vop(lambda e: e.tensor_reduce(out=xsm, in_=x3, axis=AX.X,
                                                 op=ALU.add), deps=deps + dep_bcb)
            e_arx = gop(lambda e: e.partition_all_reduce(xsm, xsm, 128,
                                                         ReduceOp.add),
                        deps=(e_xs,))
            e_sq0 = sop(lambda e: e.activation(out=rsg, in_=xbuf[:, 0:T],
                                               func=ACT.Square),
                        deps=deps + dep_bcb)
            e_add = e_sq0
            for kt in range(1, NKD):
                sl = sqb[:, (kt % 2) * TC:(kt % 2) * TC + T]
                e_sq = sop(lambda e, k=kt, o=sl: e.activation(
                    out=o, in_=xbuf[:, k * TC:k * TC + T], func=ACT.Square),
                    deps=deps + (gev(("sqf", kt % 2)),))
                e_add = vop(lambda e, i=sl: e.tensor_tensor(
                    out=rsg, in0=rsg, in1=i, op=ALU.add), deps=(e_sq, e_add))
                sev(("sqf", kt % 2), e_add)
            e_arq = gop(lambda e: e.partition_all_reduce(rsg, rsg, 128,
                                                         ReduceOp.add),
                        deps=(e_add,))
            t0 = sqb[:, 0:TC]
            t1 = sqb[:, TC:2 * TC]
            e_t = vop(lambda e: e.scalar_tensor_tensor(
                out=t0, in0=xsm, scalar=1.0 / D, in1=xsm,
                op0=ALU.mult, op1=ALU.mult),
                deps=(e_arx, gev(("sqf", 0))))
            e_v = vop(lambda e: e.tensor_tensor(out=t0, in0=rsg, in1=t0,
                                                op=ALU.subtract),
                      deps=(e_arq, e_t))
            e_v = vop(lambda e: e.tensor_scalar(out=t0, in0=t0, scalar1=1.0 / D,
                                                scalar2=EPS, op0=ALU.mult,
                                                op1=ALU.add), deps=(e_v,))
            e_l = sop(lambda e: e.activation(out=t1, in_=t0, func=ACT.Ln),
                      deps=(e_v, gev(("sqf", 1))))
            e_r0 = sop(lambda e: e.activation(out=rsg, in_=t1, func=ACT.Exp,
                                              scale=-0.5), deps=(e_l,))
            e_n = vop(lambda e: e.tensor_tensor(out=t1, in0=rsg, in1=rsg,
                                                op=ALU.mult), deps=(e_r0,))
            e_n2 = vop(lambda e: e.tensor_tensor(out=t1, in0=t1, in1=t0,
                                                 op=ALU.mult), deps=(e_n,))
            e_n3 = vop(lambda e: e.tensor_scalar(out=t1, in0=t1,
                                                 scalar1=-0.5 * scl,
                                                 scalar2=1.5 * scl,
                                                 op0=ALU.mult, op1=ALU.add),
                       deps=(e_n2,))
            ec1 = vop(lambda e: e.tensor_tensor(out=rsg, in0=rsg, in1=t1,
                                                op=ALU.mult), deps=(e_n3,))
            ec2 = vop(lambda e: e.scalar_tensor_tensor(
                out=xsm, in0=xsm, scalar=-1.0 / D, in1=rsg,
                op0=ALU.mult, op1=ALU.mult), deps=(ec1,))
            sev(("sqf", 0), e_n2)
            sev(("sqf", 1), ec1)
            return ec1, ec2

        em.op("gpsimd", lambda e: e.load_library(library_config.attn))
        ev_c = []
        for d in "fb":
            ev_c.append(em.dma("sync", cw[d][:], cwR[d], "cst"))
            ev_c.append(em.dma("sync", cb[d][:], cbR[d], "cst"))
            ev_c.append(em.dma("sync", dtpb[d][:], dtpbR[d], "cst"))
            ev_c.append(em.dma("sync", As[d][:], AR[d], "cst"))
            ev_c.append(em.dma("sync", Dsk[d][:], DskR[d], "cst"))
        ev_c.append(em.dma("sync", projb[:], projbR, "cst"))
        ev_c.append(em.dma("sync", gam[:], gamR, "cst"))
        ev_c.append(em.dma("sync", bet[:], betR, "cst"))
        evc = ev_c[-1]
        vop(lambda e: e.memset(hst["f"][:], 0.0))
        ev_hz = vop(lambda e: e.memset(hst["b"][:], 0.0))
        vop(lambda e: e.memset(halo["f"][:], 0.0))
        ev_halo = vop(lambda e: e.memset(halo["b"][:], 0.0))

        ev_As = {}
        for d in "fb":
            ea = sop(lambda e, dd=d: e.activation(out=As[dd][:], in_=As[dd][:],
                                                  func=ACT.Exp), deps=(evc,))
            ev_As[d] = vop(lambda e, dd=d: e.tensor_scalar_mul(
                out=As[dd][:], in0=As[dd][:], scalar1=-1.0), deps=(ea,))

        we_done = {}
        prjv = projTD.rearrange("(h kt p) o -> h p kt o", h=2, p=128)
        for di, d in enumerate("fb"):
            ev_p = []
            for kt in range(NKD):
                for n2 in range(2):
                    ev_p.append(em.dma("sync", prj_slice(kt, n2),
                                       prjv[di, :, kt, n2 * 512:(n2 + 1) * 512], "prj",
                                       deps=(gev("prj_readers"),)))
            ev_pl = ev_p[-1]
            for m in range(NCT):
                slot = m % 2
                src = outw[d].rearrange("(kt p) c -> p kt c", p=128)[
                    :, :, m * 128:(m + 1) * 128]
                ev_l = em.dma("sync",
                              ipslab[:, slot * 1024:(slot + 1) * 1024]
                              .rearrange("p (kt c) -> p kt c", c=128),
                              src, f"ipsl{slot}", deps=(gev(("p0slab", slot)),))
                ev_mm = None
                for n2 in range(2):
                    lst = [(ps[n2][:, 0:512],
                            ipslab[:, slot * 1024 + kt * 128:slot * 1024 + (kt + 1) * 128],
                            prj_slice(kt, n2)) for kt in range(NKD)]
                    ev_mm = mmg(lst, [n2], deps=(ev_l, ev_pl))
                    stg = e_hp[:, n2 * 512:(n2 + 1) * 512]
                    ev_cp = sop(lambda e, o=stg, i=ps[n2][:, 0:512]: e.copy(out=o, in_=i),
                                deps=(ev_mm, gev(("p0stg", n2))))
                    bank_used(n2, ev_cp)
                    dst = WeV[d][:, m, n2 * 512:(n2 + 1) * 512]
                    ev_wr = em.dma("scalar", dst, stg, f"p0st{n2}", deps=(ev_cp,))
                    sev(("p0stg", n2), ev_wr)
                    sev(("p0last", n2), ev_wr)
                    we_done[d] = ev_wr
                sev(("p0slab", slot), ev_mm)
            sev("prj_readers", ev_mm)
        for m_ in range(4):
            sev(("e_hp", m_), (gev(("p0last", 0)), gev(("p0last", 1))))
        for d in "fb":
            we_done[d] = (we_done[d], gev(("p0last", 0)), gev(("p0last", 1)))
        sev("prj_free", we_done["b"])

        sev("p1x_free", we_done["b"])
        sev("ipxn2_free", we_done["b"])
        sev("e_at_free", we_done["b"])
        sev("e_x1_free", we_done["b"])

        p1_order = []
        lo, hi = 0, NCH - 1
        while lo <= hi:
            p1_order.append(lo)
            lo += 1
            if lo <= hi:
                p1_order.append(hi)
                hi -= 1

        for j in p1_order:
            t0 = j * TC
            ev_x = em.dma("sync", p1x.ap().rearrange("p (kt t) -> p kt t", t=TC),
                          xTv[:, :, t0:t0 + TC], "p1x", deps=(gev("p1x_free"),))
            ers, enm = stats(p1x, TC, deps=(ev_x,))
            wlast = {}
            for kt in range(NKD):
                sx = p1x[:, kt * TC:(kt + 1) * TC]
                e1 = vop(lambda e, a=sx: e.tensor_tensor(out=a, in0=a, in1=bcb[:, 0:TC],
                                                         op=ALU.mult), deps=(ers,))
                e2 = vop(lambda e, a=sx: e.tensor_tensor(out=a, in0=a,
                                                         in1=bcb[:, TC:2 * TC],
                                                         op=ALU.add), deps=(enm, e1))
                ew1 = em.dma("gpsimd", xn1v[:, kt, t0:t0 + TC], sx, f"p1w{kt % 4}", deps=(e2,))
                e3 = vop(lambda e, a=sx, k=kt: e.scalar_tensor_tensor(
                    out=a, in0=a, scalar=gam[:, k:k + 1],
                    in1=bet[:, k:k + 1].broadcast_to([128, TC]),
                    op0=ALU.mult, op1=ALU.add), deps=(ew1, evc))
                ew2 = em.dma("gpsimd", xn2v[:, kt, t0:t0 + TC], sx, f"p1w{kt % 4}", deps=(e3,))
                wlast[kt % 4] = ew2
            comp = tuple(wlast.values())
            sev("p1x_free", comp)
            sev(("xn1", j), comp)
            sev(("xn2", j), comp)
            sev("bcb_free", e3)

        def in_proj_chunk(d, asc, j):
            if True:
                t0 = j * TC
                ev_xn = em.dma("sync", ipxn2.ap().rearrange("p (kt t) -> p kt t", t=TC),
                               xn2v[:, :, t0:t0 + TC], "ipxn2",
                               deps=(gev(("xn2", j)), gev("ipxn2_free")))
                ev_last = None
                for m in range(2 * NCT):
                    slot = m % 2
                    src = inwT[d].rearrange("(kt p) o -> p kt o", p=128)[
                        :, :, m * 128:(m + 1) * 128]
                    ev_l = em.dma("sync",
                                  ipslab[:, slot * 1024:(slot + 1) * 1024]
                                  .rearrange("p (kt c) -> p kt c", c=128),
                                  src, f"ipsl{slot}", deps=(gev(("ipslab", slot)),))
                    pm = ps[slot][:, 0:TC]
                    lst = [(pm, ipslab[:, slot * 1024 + kt * 128:slot * 1024 + (kt + 1) * 128],
                            ipxn2[:, kt * TC:(kt + 1) * TC]) for kt in range(NKD)]
                    ev_mm = mmg(lst, [slot], deps=(ev_l, ev_xn))
                    sev(("ipslab", slot), ev_mm)
                    ev_last = ev_mm
                    if m < NCT:
                        ct = m
                        ring = ipring[:, slot * (TC + 8):slot * (TC + 8) + TC + 8]
                        ev_ev = sop(lambda e, o=ring[:, 4:4 + TC], i=pm: e.copy(out=o, in_=i),
                                    deps=(ev_mm, gev(("ring_busy", slot))))
                        bank_used(slot, ev_ev)
                        cvt = ipcvt[:, 0:TC]
                        if asc:
                            ev_h = vop(lambda e, c=ct, rg=ring: e.tensor_copy(
                                out=rg[:, 1:4], in_=halo[d][:, c * 4:c * 4 + 3]),
                                deps=(gev(("halo", d, ct)), ev_halo,
                                      gev(("ring_busy", slot))))
                            e1 = vop(lambda e, c=ct, o=cvt, rg=ring: e.tensor_scalar(
                                out=o, in0=rg[:, 1:1 + TC],
                                scalar1=cw[d][:, c * 4:c * 4 + 1], scalar2=None,
                                op0=ALU.mult),
                                deps=(ev_ev, ev_h, gev("cvt_busy")))
                            for k in range(1, 4):
                                e1 = vop(lambda e, kk=k, c=ct, o=cvt, rg=ring:
                                         e.scalar_tensor_tensor(
                                             out=o, in0=rg[:, 1 + kk:1 + kk + TC],
                                             scalar=cw[d][:, c * 4 + kk:c * 4 + kk + 1],
                                             in1=o, op0=ALU.mult, op1=ALU.add), deps=(e1,))
                            ev_hs = vop(lambda e, c=ct, rg=ring: e.tensor_copy(
                                out=halo[d][:, c * 4:c * 4 + 3],
                                in_=rg[:, 1 + TC:4 + TC]), deps=(ev_ev,))
                        else:
                            ev_h = vop(lambda e, c=ct, rg=ring: e.tensor_copy(
                                out=rg[:, 4 + TC:4 + TC + 3],
                                in_=halo[d][:, c * 4:c * 4 + 3]),
                                deps=(gev(("halo", d, ct)), ev_halo,
                                      gev(("ring_busy", slot))))
                            e1 = vop(lambda e, c=ct, o=cvt, rg=ring: e.tensor_scalar(
                                out=o, in0=rg[:, 4:4 + TC],
                                scalar1=cw[d][:, c * 4 + 3:c * 4 + 4], scalar2=None,
                                op0=ALU.mult),
                                deps=(ev_ev, ev_h, gev("cvt_busy")))
                            for k in range(1, 4):
                                e1 = vop(lambda e, kk=k, c=ct, o=cvt, rg=ring:
                                         e.scalar_tensor_tensor(
                                             out=o, in0=rg[:, 4 + kk:4 + kk + TC],
                                             scalar=cw[d][:, c * 4 + 3 - kk:c * 4 + 4 - kk],
                                             in1=o, op0=ALU.mult, op1=ALU.add), deps=(e1,))
                            ev_hs = vop(lambda e, c=ct, rg=ring: e.tensor_copy(
                                out=halo[d][:, c * 4:c * 4 + 3], in_=rg[:, 4:7]),
                                deps=(ev_ev,))
                        sev(("halo", d, ct), ev_hs)
                        sev(("ring_busy", slot), e1)
                        uo = ipout[:, slot * TC:slot * TC + TC]
                        ev_u = sop(lambda e, c=ct, o=uo, i=cvt: e.activation(
                            out=o, in_=i, func=ACT.Silu, bias=cb[d][:, c:c + 1]),
                            deps=(e1, gev(("out_busy", slot))))
                        sev("cvt_busy", ev_u)
                        ev_w = em.dma("gpsimd", uV[d][:, ct, t0:t0 + TC], uo, f"ipo{slot}",
                                      deps=(ev_u,))
                        sev(("out_busy", slot), ev_w)
                        sev(("u", d, j, ct), ev_w)
                    else:
                        ct = m - NCT
                        zo = ipout[:, slot * TC:slot * TC + TC]
                        ev_z = sop(lambda e, o=zo, i=pm: e.activation(
                            out=o, in_=i, func=ACT.Silu),
                            deps=(ev_mm, gev(("out_busy", slot))))
                        bank_used(slot, ev_z)
                        ev_w = em.dma("gpsimd", zaV[d][:, ct, t0:t0 + TC], zo, f"ipo{slot}",
                                      deps=(ev_z,))
                        sev(("out_busy", slot), ev_w)
                        sev(("za", d, j, ct), ev_w)
                sev("ipxn2_free", ev_last)

        def xp_dtp_chunk(d, j):
            if True:
                t0 = j * TC
                ev_mm = None
                for ct in range(NCT):
                    slot = ct % 2
                    ev_l = em.dma("sync", p4u[:, slot * TC:slot * TC + TC],
                                  uV[d][:, ct, t0:t0 + TC], f"p4u{slot}",
                                  deps=(gev(("u", d, j, ct)), gev(("p4u", slot))))
                    ev_xw = em.dma("sync", p4xpw[:, slot * 96:slot * 96 + 96],
                                   xpwR[d][:, ct * 96:(ct + 1) * 96], f"p4xw{slot}",
                                   deps=(gev(("p4xpw", slot)),))
                    ev_mm = em.op("tensor",
                                  (lambda e, sl=slot, s0=(ct == 0), s1=(ct == NCT - 1):
                                   e.matmul(ps[2][0:96, 0:TC],
                                            p4xpw[:, sl * 96:sl * 96 + 96],
                                            p4u[:, sl * TC:sl * TC + TC],
                                            start=s0, stop=s1)),
                                  inc="tv",
                                  deps=((ev_l, ev_xw, bank_free[2]) if ct == 0
                                        else (ev_l, ev_xw)))
                    sev(("p4u", slot), ev_mm)
                    sev(("p4xpw", slot), ev_mm)
                ev_dt = sop(lambda e: e.copy(out=p4dt[:, 0:TC], in_=ps[2][0:DTR, 0:TC]),
                            deps=(ev_mm, gev("p4dt_free")))
                ev_bcc = sop(lambda e: e.copy(out=p4bc[DTR:96, 0:TC],
                                              in_=ps[2][DTR:96, 0:TC]),
                             deps=(ev_mm, gev("p4bc_free")))
                ev_bc = em.dma("scalar", bcSD[d][:, t0:t0 + TC], p4bc[DTR:96, 0:TC], "p4bc",
                               deps=(ev_bcc,))
                sev(("bc", d, j), ev_bc)
                sev("p4bc_free", ev_bc)
                bank_used(2, max(ev_bcc, ev_dt, key=lambda x: 0))
                ev_d = None
                for ct in range(NCT):
                    slot = ct % 2
                    ev_dw_ = em.dma("sync", p4dw[0:DTR, slot * 128:slot * 128 + 128],
                                    dtpwR[d][:, ct * 128:(ct + 1) * 128], f"p4dw{slot}",
                                    deps=(gev(("p4dw", slot)),))
                    pm = ps[3 + slot][:, 0:TC]
                    ev_d = mmg([(pm, p4dw[0:DTR, slot * 128:slot * 128 + 128],
                                 p4dt[:, 0:TC])], [3 + slot],
                               deps=(ev_dt, ev_dw_))
                    sev(("p4dw", slot), ev_d)
                    z0 = p4z0[:, slot * TC:slot * TC + TC]
                    e1 = vop(lambda e, o=z0, i=pm, c=ct: e.tensor_scalar_add(
                        out=o, in0=i, scalar1=dtpb[d][:, c:c + 1]),
                        deps=(ev_d, gev(("p4z0", slot))))
                    bank_used(3 + slot, e1)
                    e2 = sop(lambda e, i=z0: e.activation(
                        out=p4sp[:, 0:TC], in_=i, func=ACT.Abs), deps=(e1,))
                    e3 = sop(lambda e: e.activation(
                        out=p4sp[:, 0:TC], in_=p4sp[:, 0:TC], func=ACT.Exp, scale=-1.0),
                        deps=(e2,))
                    e4 = sop(lambda e: e.activation(
                        out=p4sp[:, 0:TC], in_=p4sp[:, 0:TC], func=ACT.Ln, bias=1.0),
                        deps=(e3,))
                    e5 = vop(lambda e, o=z0: e.scalar_tensor_tensor(
                        out=o, in0=o, scalar=0.0, in1=p4sp[:, 0:TC],
                        op0=ALU.max, op1=ALU.add), deps=(e4,))
                    ev_w = em.dma("gpsimd", dlV[d][:, ct, t0:t0 + TC], z0, f"p4z{slot}",
                                  deps=(e5,))
                    sev(("p4z0", slot), ev_w)
                    sev(("dl", d, j, ct), ev_w)
                sev("p4dt_free", ev_d)

        def scan_chunk(d, asc, j):
            rev = not asc
            if True:
                a0 = j * TC
                ev_B = [None] * NS
                ev_C = [None] * NS
                for n in range(NS):
                    bs = n % 4
                    srcB = bcSD[d][n:n + 1, a0:a0 + TS].partition_broadcast(128).squeeze(1)
                    ev_B[n] = em.dma("gpsimd", scB[:, n * TS:(n + 1) * TS], srcB, f"scB{bs}",
                                     deps=(gev(("bc", d, j)), gev(("scB", n))))
                    srcC = bcSD[d][NS + n:NS + n + 1, a0:a0 + TS]\
                        .partition_broadcast(128).squeeze(1)
                    ev_C[n] = em.dma("gpsimd", scC[:, n * TS:(n + 1) * TS], srcC, f"scC{bs}",
                                     deps=(gev(("bc", d, j)), gev(("scC", n))))
                for ct in range(NCT):
                    slot = ct % 2
                    ev_dl = em.dma("sync", scdl[:, slot * TS:slot * TS + TS],
                                   dlV[d][:, ct, a0:a0 + TS], f"scdl{slot}",
                                   deps=(gev(("dl", d, j, ct)), gev(("scdl", slot))))
                    ev_su = em.dma("sync", scu[:, slot * TS:slot * TS + TS],
                                   uV[d][:, ct, a0:a0 + TS], f"scu{slot}",
                                   deps=(gev(("u", d, j, ct)), gev(("scu", slot))))
                    ev_sz = em.dma("sync", scza[:, slot * TS:slot * TS + TS],
                                   zaV[d][:, ct, a0:a0 + TS], f"scza{slot}",
                                   deps=(gev(("za", d, j, ct)), gev(("scza", slot))))
                    dl = scdl[:, slot * TS:slot * TS + TS]
                    uu = scu[:, slot * TS:slot * TS + TS]
                    za = scza[:, slot * TS:slot * TS + TS]
                    ev_da = [None] * NS
                    for n in range(NS):
                        ev_da[n] = sop(lambda e, nn=n, c=ct, i=dl: e.activation(
                            out=scdA[:, nn * TS:(nn + 1) * TS], in_=i, func=ACT.Exp,
                            scale=As[d][:, c * NS + nn:c * NS + nn + 1]),
                            deps=((ev_dl, gev("scdA_free"), ev_As[d])
                                  if n == 0 else ()))
                    ev_du = vop(lambda e, i=dl, u=uu: e.tensor_tensor(
                        out=scdu[:, 0:TS], in0=i, in1=u, op=ALU.mult),
                        deps=(ev_dl, ev_su, gev("scdu_free")))
                    bh3 = scbh.ap().rearrange("p (n t) -> p n t", n=NS)
                    du3 = scdu[:, 0:TS].unsqueeze(1).broadcast_to([128, NS, TS])
                    sB3 = scB.ap().rearrange("p (n t) -> p n t", n=NS)
                    ev_b = vop(lambda e: e.tensor_tensor(
                        out=bh3, in0=du3, in1=sB3, op=ALU.mult),
                        deps=(ev_du, gev("scbh_free")) + tuple(ev_B))
                    if ct == NCT - 1:
                        for n in range(NS):
                            sev(("scB", n), ev_b)
                    ev_scan = [None] * NS
                    for n in range(NS):
                        a_sl = scdA[:, n * TS:(n + 1) * TS]
                        b_sl = scbh[:, n * TS:(n + 1) * TS]
                        if rev:
                            a_sl = a_sl[:, ::-1]
                            b_sl = b_sl[:, ::-1]
                        ev_scan[n] = vop(lambda e, a=a_sl, b=b_sl, c=ct, nn=n:
                                         e.tensor_tensor_scan(
                                             out=b, data0=a, data1=b,
                                             initial=hst[d][:, c * NS + nn:c * NS + nn + 1],
                                             op0=ALU.mult, op1=ALU.add),
                                         deps=(ev_b, ev_da[n], ev_hz))
                    bcol = 0 if rev else TS - 1
                    ev_hst = vop(lambda e, c=ct: e.tensor_copy(
                        out=hst[d][:, c * NS:(c + 1) * NS],
                        in_=scbh.ap().rearrange("p (n t) -> p n t", n=NS)[:, :, bcol]),
                        deps=(ev_scan[-1],))
                    sC3 = scC.ap().rearrange("p (n t) -> p n t", n=NS)
                    ev_hcb = vop(lambda e: e.tensor_tensor(
                        out=bh3, in0=bh3, in1=sC3, op=ALU.mult),
                        deps=(ev_scan[-1], ev_hst) + tuple(ev_C))
                    ev_hc = [ev_hcb] * NS
                    if ct == NCT - 1:
                        for n in range(NS):
                            sev(("scC", n), ev_hcb)
                    sev("scdA_free", ev_scan[-1])
                    hc_v = scbh.ap().rearrange("p (n t) -> p t n", n=NS)
                    ev_y = vop(lambda e: e.tensor_reduce(out=scy[:, 0:TS], in_=hc_v,
                                                         axis=AX.X, op=ALU.add),
                               deps=(ev_hc[-1],))
                    sev("scbh_free", ev_y)
                    ev_yp = vop(lambda e, c=ct, u=uu: e.scalar_tensor_tensor(
                        out=scy[:, 0:TS], in0=u, scalar=Dsk[d][:, c:c + 1],
                        in1=scy[:, 0:TS], op0=ALU.mult, op1=ALU.add),
                        deps=(ev_y, ev_su, evc))
                    sev("scdu_free", ev_yp)
                    sev(("scdl", slot), ev_yp)
                    sev(("scu", slot), ev_yp)
                    ys = scyp[:, 0:TS]
                    ev_ypp = vop(lambda e, o=ys, z=za: e.tensor_tensor(
                        out=o, in0=scy[:, 0:TS], in1=z, op=ALU.mult),
                        deps=(ev_yp, ev_sz, gev("scyp")))
                    sev(("scza", slot), ev_ypp)
                    ev_w = em.dma("gpsimd", ypV[d][:, ct, a0:a0 + TS], ys, "scypw",
                                  deps=(ev_ypp,))
                    sev("scyp", ev_w)
                    sev(("yp", d, j, ct), ev_w)

        def p89_mm(j):
            t0 = j * TC
            ev_m = None
            for di, d in enumerate("fb"):
                ybuf = e_yf if d == "f" else e_yb
                for ct in range(NCT):
                    slot = ct % 2
                    ev_yl = em.dma("sync", ybuf[:, slot * TC:slot * TC + TC],
                                   ypV[d][:, ct, t0:t0 + TC], f"ey{d}{slot}",
                                   deps=(gev(("yp", d, j, ct)), gev((d + "yb", slot))))
                    wslot = (di * NCT + ct) % 2
                    ev_wl = em.dma("sync", e_we[:, wslot * 1024:(wslot + 1) * 1024],
                                   WeV[d][:, ct, :], f"ewe{wslot}",
                                   deps=(we_done[d], gev(("e_we", wslot))))
                    for m in range(NKD):
                        s0 = (di == 0 and ct == 0)
                        s1 = (di == 1 and ct == NCT - 1)
                        dps = [ev_yl, ev_wl]
                        if s0:
                            dps += [bank_free[b] for b in range(NKD)]
                        ev_m = em.op("tensor",
                                     (lambda e, mm_=m, sl=slot, ws=wslot, st0=s0, st1=s1,
                                      yb=ybuf:
                                      e.matmul(ps[mm_][:, 0:TC],
                                               e_we[:, ws * 1024 + mm_ * 128:
                                                    ws * 1024 + (mm_ + 1) * 128],
                                               yb[:, sl * TC:sl * TC + TC],
                                               start=st0, stop=st1)),
                                     inc=("tv" if (s1 or m == NKD - 1) else None),
                                     deps=tuple(dps))
                    sev((d + "yb", slot), ev_m)
                    sev(("e_we", wslot), ev_m)
            ev_x0 = em.dma("sync", p1x.ap().rearrange("p (kt t) -> p kt t", t=TC),
                           xTv[:, :, t0:t0 + TC], "p1x", deps=(gev("p1x_free"),))
            sev(("p89mm", j), (ev_m, ev_x0))
            sev(("p89x0", j), ev_x0)
            sev(("p89close", j), ev_m)

        def p89_epi(j):
            t0 = j * TC
            ev_m = gev(("p89close", j))
            ev_x0 = gev(("p89x0", j))
            ev_at = []
            for m in range(NKD):
                evn = em.dma("sync", e_xn1[:, (m % 2) * TC:(m % 2) * TC + TC],
                             xn1v[:, m, t0:t0 + TC], f"exn{m % 2}",
                             deps=(gev(("xn1", j)), gev(("e_xn1", m % 2))))
                ev_at.append(vop(lambda e, mm_=m: e.scalar_tensor_tensor(
                    out=e_at[:, mm_ * TC:(mm_ + 1) * TC], in0=ps[mm_][:, 0:TC],
                    scalar=projb[:, mm_:mm_ + 1],
                    in1=e_xn1[:, (mm_ % 2) * TC:(mm_ % 2) * TC + TC],
                    op0=ALU.add, op1=ALU.add),
                    deps=(ev_m, evn, gev("e_at_free"), evc)))
                sev(("e_xn1", m % 2), ev_at[m])
                bank_used(m, ev_at[m])
            ers2, enm2 = stats(e_at, TC, deps=tuple(ev_at))
            rs3 = bcb[:, 0:TC].unsqueeze(1).broadcast_to([128, NKD, TC])
            nm3 = bcb[:, TC:2 * TC].unsqueeze(1).broadcast_to([128, NKD, TC])
            at3 = e_at.ap().rearrange("p (m t) -> p m t", m=NKD)
            x13 = e_x1.ap().rearrange("p (m t) -> p m t", m=NKD)
            x33 = e_x3.ap().rearrange("p (m t) -> p m t", m=NKD)
            e1 = vop(lambda e: e.tensor_tensor(out=x13, in0=at3, in1=rs3,
                                               op=ALU.mult),
                     deps=(ers2, gev("e_x1_free")) + tuple(ev_at))
            e2 = vop(lambda e: e.tensor_tensor(out=x13, in0=x13, in1=nm3,
                                               op=ALU.add), deps=(enm2, e1))
            e3 = vop(lambda e: e.tensor_tensor(out=e_x1[:], in0=e_x1[:],
                                               in1=p1x[:], op=ALU.add),
                     deps=(e2, ev_x0))
            ev_x1 = [e3] * NKD
            sev("e_at_free", e3)
            sev("p1x_free", e3)
            sev("bcb_free", e3)
            ers3, enm3 = stats(e_x1, TC, deps=(e3,), scl=QSCL)
            e1 = vop(lambda e: e.tensor_tensor(out=x33, in0=x13, in1=rs3,
                                               op=ALU.mult),
                     deps=(ers3, e3, gev("e_x3_free")))
            e2 = vop(lambda e: e.tensor_tensor(out=x33, in0=x33, in1=nm3,
                                               op=ALU.add), deps=(enm3, e1))
            ev_x3 = [e2] * NKD
            sev("bcb_free", e2)

            all_dn = []
            ev_hp = {}
            ev_g = None
            for m in range(NKH):
                ws = m % 2
                srcg = wgT.rearrange("(kt p) h -> p kt h", p=128)[
                    :, :, m * 128:(m + 1) * 128]
                srcu = wuT.rearrange("(kt p) h -> p kt h", p=128)[
                    :, :, m * 128:(m + 1) * 128]
                ev_wg_ = em.dma("gpsimd",
                                e_wg[:, ws * 1024:(ws + 1) * 1024]
                                .rearrange("p (kt c) -> p kt c", c=128),
                                srcg, f"ewg{ws}", deps=(gev(("e_wg", ws)),))
                ev_wu_ = em.dma("gpsimd",
                                e_wu[:, ws * 1024:(ws + 1) * 1024]
                                .rearrange("p (kt c) -> p kt c", c=128),
                                srcu, f"ewu{ws}", deps=(gev(("e_wu", ws)),))
                pgb = 0 + (m % 2)
                pub = 2 + (m % 2)
                pg = ps[pgb][:, 0:TC]
                pu = ps[pub][:, 0:TC]
                lg = [(pg, e_wg[:, ws * 1024 + kt * 128:ws * 1024 + (kt + 1) * 128],
                       e_x3[:, kt * TC:(kt + 1) * TC]) for kt in range(NKD)]
                lu = [(pu, e_wu[:, ws * 1024 + kt * 128:ws * 1024 + (kt + 1) * 128],
                       e_x3[:, kt * TC:(kt + 1) * TC]) for kt in range(NKD)]
                ev_g = mmg(lg, [pgb], deps=(ev_wg_, ev_x3[-1]))
                ev_u_ = mmg(lu, [pub], deps=(ev_wu_,))
                sev(("e_wg", ws), ev_g)
                sev(("e_wu", ws), ev_u_)
                gsl = e_gt[:, 0:TC]
                ev_si = sop(lambda e, o=gsl, i=pg: e.activation(out=o, in_=i,
                                                               func=ACT.Silu),
                            deps=(ev_g, gev("e_gt")))
                bank_used(pgb, ev_si)
                ev_h = vop(lambda e, mm_=m, g=gsl, u=pu: e.scalar_tensor_tensor(
                    out=e_hp[:, mm_ * TC:(mm_ + 1) * TC], in0=g, scalar=QSCL,
                    in1=u, op0=ALU.mult, op1=ALU.mult),
                    deps=(ev_si, ev_u_, gev(("e_hp", m))))
                bank_used(pub, ev_h)
                sev("e_gt", ev_h)
                ev_hp[m] = ev_h
            ev_dn = {}
            for sw in range(2):
                ev = None
                for m in range(NKH):
                    ws = m % 2
                    src = wdT.rearrange("(m p) o -> m p o", p=128)[
                        m, :, sw * 512:(sw + 1) * 512]
                    ev_wd_ = em.dma("gpsimd", e_wd[:, ws * 512:(ws + 1) * 512],
                                    src, f"ewd{ws}", deps=(gev(("e_wd", ws)),))
                    for m2 in range(4):
                        g = sw * 4 + m2
                        bk = 4 + m2
                        dps = [ev_wd_, ev_hp[m]]
                        if m == 0:
                            dps.append(bank_free[bk])
                        ev = em.op("tensor",
                                   (lambda e, b=bk, mm_=m, ws_=ws, c2=m2,
                                    s0=(m == 0), s1=(m == NKH - 1):
                                    e.matmul(ps[b][:, 0:TC],
                                             e_wd[:, ws_ * 512 + c2 * 128:
                                                  ws_ * 512 + c2 * 128 + 128],
                                             e_hp[:, mm_ * TC:(mm_ + 1) * TC],
                                             start=s0, stop=s1)),
                                   inc=("tv" if (m == NKH - 1 or m2 == 3) else None),
                                   deps=tuple(dps))
                        if m == NKH - 1:
                            ev_dn[g] = ev
                    sev(("e_wd", ws), ev)
                for m2 in range(4):
                    g = sw * 4 + m2
                    ev_e = sop(lambda e, gg=g, b=4 + m2: e.copy(
                        out=e_at[:, gg * TC:(gg + 1) * TC],
                        in_=ps[b][:, 0:TC]),
                        deps=(ev_dn[g], gev("e_ffn_free")))
                    bank_used(4 + m2, ev_e)
                    ev_dn[g] = ev_e
                    all_dn.append(ev_e)
            for m in range(NKH):
                sev(("e_hp", m), ev_dn[NKD - 1])
            ers4, enm4 = stats(e_at, TC, deps=tuple(all_dn))
            e1 = vop(lambda e: e.tensor_tensor(out=x33, in0=at3, in1=rs3,
                                               op=ALU.mult),
                     deps=(ers4, gev("x3_out_free"), ev_g))
            e2 = vop(lambda e: e.tensor_tensor(out=x33, in0=x33, in1=nm3,
                                               op=ALU.add), deps=(enm4, e1))
            e3 = vop(lambda e: e.tensor_tensor(out=e_x3[:], in0=e_x3[:],
                                               in1=e_x1[:], op=ALU.add),
                     deps=(e2, ev_x1[-1]))
            ev_o = [e3] * NKD
            sev("e_ffn_free", ev_o[-1])
            sev("e_x1_free", ev_o[-1])
            sev("e_at_free", ev_o[-1])
            sev("bcb_free", ev_o[-1])
            ev_out = em.dma("gpsimd", yTv[:, :, t0:t0 + TC],
                            e_x3.ap().rearrange("p (kt t) -> p kt t", t=TC), "do",
                            deps=(ev_o[-1],))
            sev("x3_out_free", ev_out)
            sev("e_x3_free", ev_out)

        for j in range(NCH):
            in_proj_chunk("f", True, j)
            xp_dtp_chunk("f", j)
            if j > 0:
                scan_chunk("f", True, j - 1)
        scan_chunk("f", True, NCH - 1)
        for j in range(NCH - 1, -1, -1):
            if j + 2 < NCH:
                p89_epi(j + 2)
            in_proj_chunk("b", False, j)
            xp_dtp_chunk("b", j)
            if j + 1 < NCH:
                scan_chunk("b", False, j + 1)
                p89_mm(j + 1)
        scan_chunk("b", False, 0)
        p89_epi(1)
        p89_mm(0)
        p89_epi(0)

        em.op("gpsimd", lambda e: e.engine_nop(),
              deps=(("do", em.cnt.get("do", 0)),))

        sems = {}
        for name in em.cnt:
            sems[name] = stack.enter_context(nc.semaphore(name))
        blk = stack.enter_context(nc.Block())
        em.replay(blk, sems)
    lower_extended_insts(nc)
    return nc


def _col_form(v, nt):
    return np.ascontiguousarray(np.asarray(v, np.float32).reshape(nt, 128).T)


def _prep_core_inputs(xb, ins):
    m = {}
    m["xT"] = np.ascontiguousarray(np.asarray(xb, np.float32).T)
    for d, sfx in (("f", "_f"), ("b", "_b")):
        m[f"inwT_{d}"] = np.ascontiguousarray(np.asarray(ins[f"L0_attn_in_w{sfx}"], np.float32).T)
        m[f"outw_{d}"] = np.ascontiguousarray(np.asarray(ins[f"L0_attn_out_w{sfx}"], np.float32))
        cwv = np.asarray(ins[f"L0_attn_conv_w{sfx}"], np.float32).reshape(DI, 4)
        m[f"cw_{d}"] = np.ascontiguousarray(
            cwv.reshape(NCT, 128, 4).transpose(1, 0, 2).reshape(128, NCT * 4))
        m[f"cb_{d}"] = _col_form(ins[f"L0_attn_conv_b{sfx}"], NCT)
        xpv = np.asarray(ins[f"L0_attn_xp_w{sfx}"], np.float32).T
        m[f"xpw_{d}"] = np.ascontiguousarray(
            xpv.reshape(NCT, 128, 96).transpose(1, 0, 2).reshape(128, NCT * 96))
        m[f"dtpw_{d}"] = np.ascontiguousarray(np.asarray(ins[f"L0_attn_dtp_w{sfx}"], np.float32).T)
        m[f"dtpb_{d}"] = _col_form(ins[f"L0_attn_dtp_b{sfx}"], NCT)
        av = np.asarray(ins[f"L0_attn_A_log{sfx}"], np.float32)
        m[f"A_{d}"] = np.ascontiguousarray(
            av.reshape(NCT, 128, NS).transpose(1, 0, 2).reshape(128, NCT * NS))
        m[f"Dsk_{d}"] = _col_form(ins[f"L0_attn_D{sfx}"], NCT)
    m["gamR"] = _col_form(ins["L0_attn_ln_gamma"], NKD)
    m["betR"] = _col_form(ins["L0_attn_ln_beta"], NKD)
    m["projT"] = np.ascontiguousarray(np.asarray(ins["L0_attn_proj_w"], np.float32).T)
    m["projbR"] = _col_form(ins["L0_attn_proj_b"], NKD)
    m["wgT"] = np.ascontiguousarray(np.asarray(ins["L0_ffn_w_gate"], np.int32).T)
    m["wuT"] = np.ascontiguousarray(np.asarray(ins["L0_ffn_w_up"], np.int32).T)
    m["wdT"] = np.ascontiguousarray(np.asarray(ins["L0_ffn_w_down"], np.int32).T)
    return m


def kernel(**inputs):
    x = np.asarray(inputs["L0_attn_x"])
    B, L, _ = x.shape
    nc = build_nc(L=L)
    in_maps = [_prep_core_inputs(x[b], inputs) for b in range(B)]
    res = run_bass_kernel_spmd(nc, in_maps, core_ids=list(range(B)))
    out = np.stack([np.asarray(r["yT"]).T for r in res.results], axis=0)
    return np.ascontiguousarray(out.astype(np.float32))


# version 19
# speedup vs baseline: 1.0007x; 1.0007x over previous
import numpy as np
import concourse.bass as bass
import concourse.mybir as mybir
from concourse.bass_utils import run_bass_kernel_spmd
from concourse.bass_isa import ReduceOp
from concourse import library_config
from concourse.library_overlay import lower_extended_insts
from contextlib import ExitStack

F32 = mybir.dt.float32
I32 = mybir.dt.int32
ALU = mybir.AluOpType
ACT = mybir.ActivationFunctionType
AX = mybir.AxisListType

D = 1024
DI = 2048
NS = 16
DTR = 64
HID = 3072
NKD = D // 128
NCT = DI // 128
NKH = HID // 128
EPS = 1e-5
QSCL = float(np.float32(1.0 / 127.0))

ENGINES = ("sync", "tensor", "vector", "scalar", "gpsimd")


class EM:
    def __init__(self):
        self.q = {e: [] for e in ENGINES}
        self.cnt = {}
        self.mark = {}

    def op(self, eng, fn, inc=None, deps=(), amt=1):
        flat = []
        for dp in deps:
            if dp is None:
                continue
            if isinstance(dp, (list, tuple)) and dp and isinstance(dp[0], (list, tuple)):
                flat.extend(x for x in dp if x is not None)
            else:
                flat.append(dp)
        for dp in flat:
            s, v = dp
            if self.mark.get((eng, s), 0) < v:
                self.q[eng].append(("w", s, v))
                self.mark[(eng, s)] = v
        ev = None
        if inc is not None:
            self.cnt[inc] = self.cnt.get(inc, 0) + amt
            ev = (inc, self.cnt[inc])
        self.q[eng].append(("o", fn, inc, amt))
        return ev

    def dma(self, eng, out, in_, sem, deps=()):
        prior = self.cnt.get(sem, 0)
        return self.op(eng, lambda e, o=out, i=in_: e.dma_start(out=o, in_=i),
                       inc=sem, deps=deps + ((sem, prior),), amt=16)

    def replay(self, block, sems):
        for name in ENGINES:
            section = getattr(block, name)

            def make(qname):
                def body(eng):
                    for item in self.q[qname]:
                        if item[0] == "w":
                            eng.wait_ge(sems[item[1]], item[2])
                        else:
                            inst = item[1](eng)
                            if item[2] is not None:
                                inst.then_inc(sems[item[2]], item[3])
                return body

            section(make(name))


def build_nc(L=4096, taps=False):
    TC = 256
    NCH = L // TC
    TS = TC
    TC9 = 128

    nc = bass.Bass("TRN2", target_bir_lowering=False, debug=False)

    dram = lambda n, s, dt=F32, k="ExternalInput": nc.dram_tensor(n, s, dt, kind=k).ap()
    xT = dram("xT", [D, L])
    inwT = {d: dram(f"inwT_{d}", [D, 2 * DI]) for d in "fb"}
    outw = {d: dram(f"outw_{d}", [D, DI]) for d in "fb"}
    cwR = {d: dram(f"cw_{d}", [128, NCT * 4]) for d in "fb"}
    cbR = {d: dram(f"cb_{d}", [128, NCT]) for d in "fb"}
    xpwR = {d: dram(f"xpw_{d}", [128, NCT * 96]) for d in "fb"}
    dtpwR = {d: dram(f"dtpw_{d}", [DTR, DI]) for d in "fb"}
    dtpbR = {d: dram(f"dtpb_{d}", [128, NCT]) for d in "fb"}
    AR = {d: dram(f"A_{d}", [128, NCT * NS]) for d in "fb"}
    DskR = {d: dram(f"Dsk_{d}", [128, NCT]) for d in "fb"}
    gamR = dram("gamR", [128, NKD])
    betR = dram("betR", [128, NKD])
    projTD = dram("projT", [2 * D, D])
    projbR = dram("projbR", [128, NKD])
    wgT = dram("wgT", [D, HID], I32)
    wuT = dram("wuT", [D, HID], I32)
    wdT = dram("wdT", [HID, D], I32)

    ks = "ExternalOutput" if taps else "Internal"
    WeT = {d: dram(f"WeT_{d}", [DI, D], F32, ks) for d in "fb"}
    xn1S = dram("xn1S", [D, L], F32, ks)
    xn2S = dram("xn2S", [D, L], F32, ks)
    uS = {d: dram(f"uS_{d}", [DI, L], F32, ks) for d in "fb"}
    zaS = {d: dram(f"zaS_{d}", [DI, L], F32, ks) for d in "fb"}
    dlS = {d: dram(f"dlS_{d}", [DI, L], F32, ks) for d in "fb"}
    bcSD = {d: dram(f"bcS_{d}", [32, L], F32, ks) for d in "fb"}
    ypS = {d: dram(f"ypS_{d}", [DI, L], F32, ks) for d in "fb"}
    yT = dram("yT", [D, L], F32, "ExternalOutput")

    def dview(t, p=128):
        return t.rearrange("(a p) t -> p a t", p=p)
    xTv = dview(xT)
    xn1v = dview(xn1S)
    xn2v = dview(xn2S)
    yTv = dview(yT)
    uV = {d: dview(uS[d]) for d in "fb"}
    zaV = {d: dview(zaS[d]) for d in "fb"}
    dlV = {d: dview(dlS[d]) for d in "fb"}
    ypV = {d: dview(ypS[d]) for d in "fb"}
    WeV = {d: dview(WeT[d]) for d in "fb"}

    em = EM()
    st = {}
    sev = st.__setitem__
    gev = st.get

    stack = ExitStack()
    sb = lambda name, shape, dt=F32: stack.enter_context(nc.sbuf_tensor(name, shape, dt))
    psmk = lambda name, shape: stack.enter_context(nc.psum_tensor(name, shape, F32))

    with stack:
        cw = {d: sb(f"cw{d}", [128, NCT * 4]) for d in "fb"}
        cb = {d: sb(f"cb{d}", [128, NCT]) for d in "fb"}
        dtpb = {d: sb(f"dtpb{d}", [128, NCT]) for d in "fb"}
        As = {d: sb(f"As{d}", [128, NCT * NS]) for d in "fb"}
        Dsk = {d: sb(f"Dsk{d}", [128, NCT]) for d in "fb"}
        projb = sb("projb", [128, NKD])
        gam = sb("gam", [128, NKD])
        bet = sb("bet", [128, NKD])
        hst = {d: sb(f"hst{d}", [128, NCT * NS]) for d in "fb"}
        halo = {d: sb(f"halo{d}", [128, NCT * 4]) for d in "fb"}

        p1x = sb("p1x", [128, NKD * TC])
        sqb = sb("sqb", [128, 2 * TC])
        bcb = sb("bcb", [128, 2 * TC])

        ipslab = sb("ipslab", [128, 2 * 1024])
        ipxn2 = sb("ipxn2", [128, NKD * TC])
        ipring = sb("ipring", [128, 2 * (TC + 8)])
        ipcvt = sb("ipcvt", [128, TC])
        ipout = sb("ipout", [128, 2 * TC])

        p4u = sb("p4u", [128, 2 * TC])
        p4dt = sb("p4dt", [64, TC])
        p4z0 = sb("p4z0", [128, 2 * TC])
        p4sp = sb("p4sp", [128, TC])
        p4xpw = sb("p4xpw", [128, 2 * 96])
        p4bc = sb("p4bc", [128, TC])
        p4dw = sb("p4dw", [64, 2 * 128])

        scdl = sb("scdl", [128, 2 * TS])
        scu = sb("scu", [128, 2 * TS])
        scza = sb("scza", [128, 2 * TS])
        scdA = sb("scdA", [128, NS * TS])
        scbh = sb("scbh", [128, NS * TS])
        scB = sb("scB", [128, NS * TS])
        scC = sb("scC", [128, NS * TS])
        scdu = sb("scdu", [128, TS])
        scy = sb("scy", [128, TS])
        scyp = sb("scyp", [128, TS])

        e_yf = sb("e_yf", [128, 2 * TC])
        e_yb = sb("e_yb", [128, 2 * TC])
        e_we = sb("e_we", [128, 2 * 1024])
        e_xn1 = sb("e_xn1", [128, 2 * TC])
        e_at = sb("e_at", [128, NKD * TC])
        e_x1 = sb("e_x1", [128, NKD * TC])
        e_x3 = sb("e_x3", [128, NKD * TC])
        e_hp = sb("e_hp", [128, NKH * TC])
        e_gt = sb("e_gt", [128, 2 * TC])
        e_wg = sb("e_wg", [128, 2 * 1024])
        e_wu = sb("e_wu", [128, 2 * 1024])
        e_wd = sb("e_wd", [128, 2 * 512])

        ps = [psmk(f"ps{i}", [128, 512]) for i in range(8)]

        blk_holder = []

        def prj_slice(kt, n2):
            buf = (ipxn2, p1x, e_at, e_x1)[kt // 2]
            off = (kt % 2) * 1024 + n2 * 512
            return buf[:, off:off + 512]

        def vop(fn, deps=(), inc="vv"):
            return em.op("vector", fn, inc=inc, deps=deps)

        def sop(fn, deps=(), inc="sv"):
            return em.op("scalar", fn, inc=inc, deps=deps)

        bank_free = {i: None for i in range(8)}

        def mmg(lst, banks, deps=(), inc="tv"):
            bdeps = tuple(bank_free[b] for b in banks)
            ev = None
            n = len(lst)
            for i, (o, l, r) in enumerate(lst):
                ev = em.op("tensor",
                           (lambda e, oo=o, ll=l, rr=r, s0=(i == 0), s1=(i == n - 1):
                            e.matmul(oo, ll, rr, start=s0, stop=s1)),
                           inc=(inc if i == n - 1 else None),
                           deps=(deps + bdeps) if i == 0 else ())
            return ev

        def bank_used(b, ev):
            bank_free[b] = ev

        def gop(fn, deps=(), inc="gv"):
            return em.op("gpsimd", fn, inc=inc, deps=deps)

        def stats(xbuf, T, deps, scl=1.0):
            assert T == TC
            dep_bcb = (gev("bcb_free"),)
            rsg = bcb[:, 0:T]
            xsm = bcb[:, TC:TC + T]
            x3 = xbuf[:].rearrange("p (a t) -> p t a", a=NKD)
            e_xs = vop(lambda e: e.tensor_reduce(out=xsm, in_=x3, axis=AX.X,
                                                 op=ALU.add), deps=deps + dep_bcb)
            e_arx = gop(lambda e: e.partition_all_reduce(xsm, xsm, 128,
                                                         ReduceOp.add),
                        deps=(e_xs,))
            e_sq0 = sop(lambda e: e.activation(out=rsg, in_=xbuf[:, 0:T],
                                               func=ACT.Square),
                        deps=deps + dep_bcb)
            e_add = e_sq0
            for kt in range(1, NKD):
                sl = sqb[:, (kt % 2) * TC:(kt % 2) * TC + T]
                e_sq = sop(lambda e, k=kt, o=sl: e.activation(
                    out=o, in_=xbuf[:, k * TC:k * TC + T], func=ACT.Square),
                    deps=deps + (gev(("sqf", kt % 2)),))
                e_add = vop(lambda e, i=sl: e.tensor_tensor(
                    out=rsg, in0=rsg, in1=i, op=ALU.add), deps=(e_sq, e_add))
                sev(("sqf", kt % 2), e_add)
            e_arq = gop(lambda e: e.partition_all_reduce(rsg, rsg, 128,
                                                         ReduceOp.add),
                        deps=(e_add,))
            t0 = sqb[:, 0:TC]
            t1 = sqb[:, TC:2 * TC]
            e_t = vop(lambda e: e.scalar_tensor_tensor(
                out=t0, in0=xsm, scalar=1.0 / D, in1=xsm,
                op0=ALU.mult, op1=ALU.mult),
                deps=(e_arx, gev(("sqf", 0))))
            e_v = vop(lambda e: e.tensor_tensor(out=t0, in0=rsg, in1=t0,
                                                op=ALU.subtract),
                      deps=(e_arq, e_t))
            e_v = vop(lambda e: e.tensor_scalar(out=t0, in0=t0, scalar1=1.0 / D,
                                                scalar2=EPS, op0=ALU.mult,
                                                op1=ALU.add), deps=(e_v,))
            e_l = sop(lambda e: e.activation(out=t1, in_=t0, func=ACT.Ln),
                      deps=(e_v, gev(("sqf", 1))))
            e_r0 = sop(lambda e: e.activation(out=rsg, in_=t1, func=ACT.Exp,
                                              scale=-0.5), deps=(e_l,))
            e_n = vop(lambda e: e.tensor_tensor(out=t1, in0=rsg, in1=rsg,
                                                op=ALU.mult), deps=(e_r0,))
            e_n2 = vop(lambda e: e.tensor_tensor(out=t1, in0=t1, in1=t0,
                                                 op=ALU.mult), deps=(e_n,))
            e_n3 = vop(lambda e: e.tensor_scalar(out=t1, in0=t1,
                                                 scalar1=-0.5 * scl,
                                                 scalar2=1.5 * scl,
                                                 op0=ALU.mult, op1=ALU.add),
                       deps=(e_n2,))
            ec1 = vop(lambda e: e.tensor_tensor(out=rsg, in0=rsg, in1=t1,
                                                op=ALU.mult), deps=(e_n3,))
            ec2 = vop(lambda e: e.scalar_tensor_tensor(
                out=xsm, in0=xsm, scalar=-1.0 / D, in1=rsg,
                op0=ALU.mult, op1=ALU.mult), deps=(ec1,))
            sev(("sqf", 0), e_n2)
            sev(("sqf", 1), ec1)
            return ec1, ec2

        em.op("gpsimd", lambda e: e.load_library(library_config.attn))
        ev_c = []
        for d in "fb":
            ev_c.append(em.dma("sync", cw[d][:], cwR[d], "cst"))
            ev_c.append(em.dma("sync", cb[d][:], cbR[d], "cst"))
            ev_c.append(em.dma("sync", dtpb[d][:], dtpbR[d], "cst"))
            ev_c.append(em.dma("sync", As[d][:], AR[d], "cst"))
            ev_c.append(em.dma("sync", Dsk[d][:], DskR[d], "cst"))
        ev_c.append(em.dma("sync", projb[:], projbR, "cst"))
        ev_c.append(em.dma("sync", gam[:], gamR, "cst"))
        ev_c.append(em.dma("sync", bet[:], betR, "cst"))
        evc = ev_c[-1]
        vop(lambda e: e.memset(hst["f"][:], 0.0))
        ev_hz = vop(lambda e: e.memset(hst["b"][:], 0.0))
        vop(lambda e: e.memset(halo["f"][:], 0.0))
        ev_halo = vop(lambda e: e.memset(halo["b"][:], 0.0))

        ev_As = {}
        for d in "fb":
            ea = sop(lambda e, dd=d: e.activation(out=As[dd][:], in_=As[dd][:],
                                                  func=ACT.Exp), deps=(evc,))
            ev_As[d] = vop(lambda e, dd=d: e.tensor_scalar_mul(
                out=As[dd][:], in0=As[dd][:], scalar1=-1.0), deps=(ea,))

        we_done = {}
        prjv = projTD.rearrange("(h kt p) o -> h p kt o", h=2, p=128)
        for di, d in enumerate("fb"):
            ev_p = []
            for kt in range(NKD):
                for n2 in range(2):
                    ev_p.append(em.dma("sync", prj_slice(kt, n2),
                                       prjv[di, :, kt, n2 * 512:(n2 + 1) * 512], "prj",
                                       deps=(gev("prj_readers"),)))
            ev_pl = ev_p[-1]
            for m in range(NCT):
                slot = m % 2
                src = outw[d].rearrange("(kt p) c -> p kt c", p=128)[
                    :, :, m * 128:(m + 1) * 128]
                ev_l = em.dma("sync",
                              ipslab[:, slot * 1024:(slot + 1) * 1024]
                              .rearrange("p (kt c) -> p kt c", c=128),
                              src, f"ipsl{slot}", deps=(gev(("p0slab", slot)),))
                ev_mm = None
                for n2 in range(2):
                    lst = [(ps[n2][:, 0:512],
                            ipslab[:, slot * 1024 + kt * 128:slot * 1024 + (kt + 1) * 128],
                            prj_slice(kt, n2)) for kt in range(NKD)]
                    ev_mm = mmg(lst, [n2], deps=(ev_l, ev_pl))
                    stg = e_hp[:, n2 * 512:(n2 + 1) * 512]
                    ev_cp = sop(lambda e, o=stg, i=ps[n2][:, 0:512]: e.copy(out=o, in_=i),
                                deps=(ev_mm, gev(("p0stg", n2))))
                    bank_used(n2, ev_cp)
                    dst = WeV[d][:, m, n2 * 512:(n2 + 1) * 512]
                    ev_wr = em.dma("scalar", dst, stg, f"p0st{n2}", deps=(ev_cp,))
                    sev(("p0stg", n2), ev_wr)
                    sev(("p0last", n2), ev_wr)
                    we_done[d] = ev_wr
                sev(("p0slab", slot), ev_mm)
            sev("prj_readers", ev_mm)
        for m_ in range(4):
            sev(("e_hp", m_), (gev(("p0last", 0)), gev(("p0last", 1))))
        for d in "fb":
            we_done[d] = (we_done[d], gev(("p0last", 0)), gev(("p0last", 1)))
        sev("prj_free", we_done["b"])

        sev("p1x_free", we_done["b"])
        sev("ipxn2_free", we_done["b"])
        sev("e_at_free", we_done["b"])
        sev("e_x1_free", we_done["b"])

        p1_order = []
        lo, hi = 0, NCH - 1
        while lo <= hi:
            p1_order.append(lo)
            lo += 1
            if lo <= hi:
                p1_order.append(hi)
                hi -= 1

        for j in p1_order:
            t0 = j * TC
            ev_x = em.dma("sync", p1x.ap().rearrange("p (kt t) -> p kt t", t=TC),
                          xTv[:, :, t0:t0 + TC], "p1x", deps=(gev("p1x_free"),))
            ers, enm = stats(p1x, TC, deps=(ev_x,))
            wlast = {}
            for kt in range(NKD):
                sx = p1x[:, kt * TC:(kt + 1) * TC]
                e1 = vop(lambda e, a=sx: e.tensor_tensor(out=a, in0=a, in1=bcb[:, 0:TC],
                                                         op=ALU.mult), deps=(ers,))
                e2 = vop(lambda e, a=sx: e.tensor_tensor(out=a, in0=a,
                                                         in1=bcb[:, TC:2 * TC],
                                                         op=ALU.add), deps=(enm, e1))
                ew1 = em.dma("gpsimd", xn1v[:, kt, t0:t0 + TC], sx, f"p1w{kt % 4}", deps=(e2,))
                e3 = vop(lambda e, a=sx, k=kt: e.scalar_tensor_tensor(
                    out=a, in0=a, scalar=gam[:, k:k + 1],
                    in1=bet[:, k:k + 1].broadcast_to([128, TC]),
                    op0=ALU.mult, op1=ALU.add), deps=(ew1, evc))
                ew2 = em.dma("gpsimd", xn2v[:, kt, t0:t0 + TC], sx, f"p1w{kt % 4}", deps=(e3,))
                wlast[kt % 4] = ew2
            comp = tuple(wlast.values())
            sev("p1x_free", comp)
            sev(("xn1", j), comp)
            sev(("xn2", j), comp)
            sev("bcb_free", e3)

        def in_proj_chunk(d, asc, j):
            if True:
                t0 = j * TC
                ev_xn = em.dma("sync", ipxn2.ap().rearrange("p (kt t) -> p kt t", t=TC),
                               xn2v[:, :, t0:t0 + TC], "ipxn2",
                               deps=(gev(("xn2", j)), gev("ipxn2_free")))
                ev_last = None
                for m in range(2 * NCT):
                    slot = m % 2
                    src = inwT[d].rearrange("(kt p) o -> p kt o", p=128)[
                        :, :, m * 128:(m + 1) * 128]
                    ev_l = em.dma("sync",
                                  ipslab[:, slot * 1024:(slot + 1) * 1024]
                                  .rearrange("p (kt c) -> p kt c", c=128),
                                  src, f"ipsl{slot}", deps=(gev(("ipslab", slot)),))
                    pm = ps[slot][:, 0:TC]
                    lst = [(pm, ipslab[:, slot * 1024 + kt * 128:slot * 1024 + (kt + 1) * 128],
                            ipxn2[:, kt * TC:(kt + 1) * TC]) for kt in range(NKD)]
                    ev_mm = mmg(lst, [slot], deps=(ev_l, ev_xn))
                    sev(("ipslab", slot), ev_mm)
                    ev_last = ev_mm
                    if m < NCT:
                        ct = m
                        ring = ipring[:, slot * (TC + 8):slot * (TC + 8) + TC + 8]
                        ev_ev = sop(lambda e, o=ring[:, 4:4 + TC], i=pm: e.copy(out=o, in_=i),
                                    deps=(ev_mm, gev(("ring_busy", slot))))
                        bank_used(slot, ev_ev)
                        cvt = ipcvt[:, 0:TC]
                        if asc:
                            ev_h = vop(lambda e, c=ct, rg=ring: e.tensor_copy(
                                out=rg[:, 1:4], in_=halo[d][:, c * 4:c * 4 + 3]),
                                deps=(gev(("halo", d, ct)), ev_halo,
                                      gev(("ring_busy", slot))))
                            e1 = vop(lambda e, c=ct, o=cvt, rg=ring: e.tensor_scalar(
                                out=o, in0=rg[:, 1:1 + TC],
                                scalar1=cw[d][:, c * 4:c * 4 + 1], scalar2=None,
                                op0=ALU.mult),
                                deps=(ev_ev, ev_h, gev("cvt_busy")))
                            for k in range(1, 4):
                                e1 = vop(lambda e, kk=k, c=ct, o=cvt, rg=ring:
                                         e.scalar_tensor_tensor(
                                             out=o, in0=rg[:, 1 + kk:1 + kk + TC],
                                             scalar=cw[d][:, c * 4 + kk:c * 4 + kk + 1],
                                             in1=o, op0=ALU.mult, op1=ALU.add), deps=(e1,))
                            ev_hs = vop(lambda e, c=ct, rg=ring: e.tensor_copy(
                                out=halo[d][:, c * 4:c * 4 + 3],
                                in_=rg[:, 1 + TC:4 + TC]), deps=(ev_ev,))
                        else:
                            ev_h = vop(lambda e, c=ct, rg=ring: e.tensor_copy(
                                out=rg[:, 4 + TC:4 + TC + 3],
                                in_=halo[d][:, c * 4:c * 4 + 3]),
                                deps=(gev(("halo", d, ct)), ev_halo,
                                      gev(("ring_busy", slot))))
                            e1 = vop(lambda e, c=ct, o=cvt, rg=ring: e.tensor_scalar(
                                out=o, in0=rg[:, 4:4 + TC],
                                scalar1=cw[d][:, c * 4 + 3:c * 4 + 4], scalar2=None,
                                op0=ALU.mult),
                                deps=(ev_ev, ev_h, gev("cvt_busy")))
                            for k in range(1, 4):
                                e1 = vop(lambda e, kk=k, c=ct, o=cvt, rg=ring:
                                         e.scalar_tensor_tensor(
                                             out=o, in0=rg[:, 4 + kk:4 + kk + TC],
                                             scalar=cw[d][:, c * 4 + 3 - kk:c * 4 + 4 - kk],
                                             in1=o, op0=ALU.mult, op1=ALU.add), deps=(e1,))
                            ev_hs = vop(lambda e, c=ct, rg=ring: e.tensor_copy(
                                out=halo[d][:, c * 4:c * 4 + 3], in_=rg[:, 4:7]),
                                deps=(ev_ev,))
                        sev(("halo", d, ct), ev_hs)
                        sev(("ring_busy", slot), e1)
                        uo = ipout[:, slot * TC:slot * TC + TC]
                        ev_u = sop(lambda e, c=ct, o=uo, i=cvt: e.activation(
                            out=o, in_=i, func=ACT.Silu, bias=cb[d][:, c:c + 1]),
                            deps=(e1, gev(("out_busy", slot))))
                        sev("cvt_busy", ev_u)
                        ev_w = em.dma("gpsimd", uV[d][:, ct, t0:t0 + TC], uo, f"ipo{slot}",
                                      deps=(ev_u,))
                        sev(("out_busy", slot), ev_w)
                        sev(("u", d, j, ct), ev_w)
                    else:
                        ct = m - NCT
                        zo = ipout[:, slot * TC:slot * TC + TC]
                        ev_z = sop(lambda e, o=zo, i=pm: e.activation(
                            out=o, in_=i, func=ACT.Silu),
                            deps=(ev_mm, gev(("out_busy", slot))))
                        bank_used(slot, ev_z)
                        ev_w = em.dma("gpsimd", zaV[d][:, ct, t0:t0 + TC], zo, f"ipo{slot}",
                                      deps=(ev_z,))
                        sev(("out_busy", slot), ev_w)
                        sev(("za", d, j, ct), ev_w)
                sev("ipxn2_free", ev_last)

        def xp_dtp_chunk(d, j):
            if True:
                t0 = j * TC
                ev_mm = None
                for ct in range(NCT):
                    slot = ct % 2
                    ev_l = em.dma("sync", p4u[:, slot * TC:slot * TC + TC],
                                  uV[d][:, ct, t0:t0 + TC], f"p4u{slot}",
                                  deps=(gev(("u", d, j, ct)), gev(("p4u", slot))))
                    ev_xw = em.dma("sync", p4xpw[:, slot * 96:slot * 96 + 96],
                                   xpwR[d][:, ct * 96:(ct + 1) * 96], f"p4xw{slot}",
                                   deps=(gev(("p4xpw", slot)),))
                    ev_mm = em.op("tensor",
                                  (lambda e, sl=slot, s0=(ct == 0), s1=(ct == NCT - 1):
                                   e.matmul(ps[2][0:96, 0:TC],
                                            p4xpw[:, sl * 96:sl * 96 + 96],
                                            p4u[:, sl * TC:sl * TC + TC],
                                            start=s0, stop=s1)),
                                  inc="tv",
                                  deps=((ev_l, ev_xw, bank_free[2]) if ct == 0
                                        else (ev_l, ev_xw)))
                    sev(("p4u", slot), ev_mm)
                    sev(("p4xpw", slot), ev_mm)
                ev_dt = sop(lambda e: e.copy(out=p4dt[:, 0:TC], in_=ps[2][0:DTR, 0:TC]),
                            deps=(ev_mm, gev("p4dt_free")))
                ev_bcc = sop(lambda e: e.copy(out=p4bc[DTR:96, 0:TC],
                                              in_=ps[2][DTR:96, 0:TC]),
                             deps=(ev_mm, gev("p4bc_free")))
                ev_bc = em.dma("scalar", bcSD[d][:, t0:t0 + TC], p4bc[DTR:96, 0:TC], "p4bc",
                               deps=(ev_bcc,))
                sev(("bc", d, j), ev_bc)
                sev("p4bc_free", ev_bc)
                bank_used(2, max(ev_bcc, ev_dt, key=lambda x: 0))
                ev_d = None
                for ct in range(NCT):
                    slot = ct % 2
                    ev_dw_ = em.dma("sync", p4dw[0:DTR, slot * 128:slot * 128 + 128],
                                    dtpwR[d][:, ct * 128:(ct + 1) * 128], f"p4dw{slot}",
                                    deps=(gev(("p4dw", slot)),))
                    pm = ps[3 + slot][:, 0:TC]
                    ev_d = mmg([(pm, p4dw[0:DTR, slot * 128:slot * 128 + 128],
                                 p4dt[:, 0:TC])], [3 + slot],
                               deps=(ev_dt, ev_dw_))
                    sev(("p4dw", slot), ev_d)
                    z0 = p4z0[:, slot * TC:slot * TC + TC]
                    e1 = vop(lambda e, o=z0, i=pm, c=ct: e.tensor_scalar_add(
                        out=o, in0=i, scalar1=dtpb[d][:, c:c + 1]),
                        deps=(ev_d, gev(("p4z0", slot))))
                    bank_used(3 + slot, e1)
                    e2 = sop(lambda e, i=z0: e.activation(
                        out=p4sp[:, 0:TC], in_=i, func=ACT.Abs), deps=(e1,))
                    e3 = sop(lambda e: e.activation(
                        out=p4sp[:, 0:TC], in_=p4sp[:, 0:TC], func=ACT.Exp, scale=-1.0),
                        deps=(e2,))
                    e4 = sop(lambda e: e.activation(
                        out=p4sp[:, 0:TC], in_=p4sp[:, 0:TC], func=ACT.Ln, bias=1.0),
                        deps=(e3,))
                    e5 = vop(lambda e, o=z0: e.scalar_tensor_tensor(
                        out=o, in0=o, scalar=0.0, in1=p4sp[:, 0:TC],
                        op0=ALU.max, op1=ALU.add), deps=(e4,))
                    ev_w = em.dma("gpsimd", dlV[d][:, ct, t0:t0 + TC], z0, f"p4z{slot}",
                                  deps=(e5,))
                    sev(("p4z0", slot), ev_w)
                    sev(("dl", d, j, ct), ev_w)
                sev("p4dt_free", ev_d)

        def scan_chunk(d, asc, j):
            rev = not asc
            if True:
                a0 = j * TC
                ev_B = [None] * NS
                ev_C = [None] * NS
                for n in range(NS):
                    bs = n % 4
                    srcB = bcSD[d][n:n + 1, a0:a0 + TS].partition_broadcast(128).squeeze(1)
                    ev_B[n] = em.dma("scalar", scB[:, n * TS:(n + 1) * TS], srcB, f"scB{bs}",
                                     deps=(gev(("bc", d, j)), gev(("scB", n))))
                    srcC = bcSD[d][NS + n:NS + n + 1, a0:a0 + TS]\
                        .partition_broadcast(128).squeeze(1)
                    ev_C[n] = em.dma("scalar", scC[:, n * TS:(n + 1) * TS], srcC, f"scC{bs}",
                                     deps=(gev(("bc", d, j)), gev(("scC", n))))
                for ct in range(NCT):
                    slot = ct % 2
                    ev_dl = em.dma("sync", scdl[:, slot * TS:slot * TS + TS],
                                   dlV[d][:, ct, a0:a0 + TS], f"scdl{slot}",
                                   deps=(gev(("dl", d, j, ct)), gev(("scdl", slot))))
                    ev_su = em.dma("sync", scu[:, slot * TS:slot * TS + TS],
                                   uV[d][:, ct, a0:a0 + TS], f"scu{slot}",
                                   deps=(gev(("u", d, j, ct)), gev(("scu", slot))))
                    ev_sz = em.dma("sync", scza[:, slot * TS:slot * TS + TS],
                                   zaV[d][:, ct, a0:a0 + TS], f"scza{slot}",
                                   deps=(gev(("za", d, j, ct)), gev(("scza", slot))))
                    dl = scdl[:, slot * TS:slot * TS + TS]
                    uu = scu[:, slot * TS:slot * TS + TS]
                    za = scza[:, slot * TS:slot * TS + TS]
                    ev_da = [None] * NS
                    for n in range(NS):
                        ev_da[n] = sop(lambda e, nn=n, c=ct, i=dl: e.activation(
                            out=scdA[:, nn * TS:(nn + 1) * TS], in_=i, func=ACT.Exp,
                            scale=As[d][:, c * NS + nn:c * NS + nn + 1]),
                            deps=((ev_dl, gev("scdA_free"), ev_As[d])
                                  if n == 0 else ()))
                    ev_du = vop(lambda e, i=dl, u=uu: e.tensor_tensor(
                        out=scdu[:, 0:TS], in0=i, in1=u, op=ALU.mult),
                        deps=(ev_dl, ev_su, gev("scdu_free")))
                    bh3 = scbh.ap().rearrange("p (n t) -> p n t", n=NS)
                    du3 = scdu[:, 0:TS].unsqueeze(1).broadcast_to([128, NS, TS])
                    sB3 = scB.ap().rearrange("p (n t) -> p n t", n=NS)
                    ev_b = vop(lambda e: e.tensor_tensor(
                        out=bh3, in0=du3, in1=sB3, op=ALU.mult),
                        deps=(ev_du, gev("scbh_free")) + tuple(ev_B))
                    if ct == NCT - 1:
                        for n in range(NS):
                            sev(("scB", n), ev_b)
                    ev_scan = [None] * NS
                    for n in range(NS):
                        a_sl = scdA[:, n * TS:(n + 1) * TS]
                        b_sl = scbh[:, n * TS:(n + 1) * TS]
                        if rev:
                            a_sl = a_sl[:, ::-1]
                            b_sl = b_sl[:, ::-1]
                        ev_scan[n] = vop(lambda e, a=a_sl, b=b_sl, c=ct, nn=n:
                                         e.tensor_tensor_scan(
                                             out=b, data0=a, data1=b,
                                             initial=hst[d][:, c * NS + nn:c * NS + nn + 1],
                                             op0=ALU.mult, op1=ALU.add),
                                         deps=(ev_b, ev_da[n], ev_hz))
                    bcol = 0 if rev else TS - 1
                    ev_hst = vop(lambda e, c=ct: e.tensor_copy(
                        out=hst[d][:, c * NS:(c + 1) * NS],
                        in_=scbh.ap().rearrange("p (n t) -> p n t", n=NS)[:, :, bcol]),
                        deps=(ev_scan[-1],))
                    sC3 = scC.ap().rearrange("p (n t) -> p n t", n=NS)
                    ev_hcb = vop(lambda e: e.tensor_tensor(
                        out=bh3, in0=bh3, in1=sC3, op=ALU.mult),
                        deps=(ev_scan[-1], ev_hst) + tuple(ev_C))
                    ev_hc = [ev_hcb] * NS
                    if ct == NCT - 1:
                        for n in range(NS):
                            sev(("scC", n), ev_hcb)
                    sev("scdA_free", ev_scan[-1])
                    hc_v = scbh.ap().rearrange("p (n t) -> p t n", n=NS)
                    ev_y = vop(lambda e: e.tensor_reduce(out=scy[:, 0:TS], in_=hc_v,
                                                         axis=AX.X, op=ALU.add),
                               deps=(ev_hc[-1],))
                    sev("scbh_free", ev_y)
                    ev_yp = vop(lambda e, c=ct, u=uu: e.scalar_tensor_tensor(
                        out=scy[:, 0:TS], in0=u, scalar=Dsk[d][:, c:c + 1],
                        in1=scy[:, 0:TS], op0=ALU.mult, op1=ALU.add),
                        deps=(ev_y, ev_su, evc))
                    sev("scdu_free", ev_yp)
                    sev(("scdl", slot), ev_yp)
                    sev(("scu", slot), ev_yp)
                    ys = scyp[:, 0:TS]
                    ev_ypp = vop(lambda e, o=ys, z=za: e.tensor_tensor(
                        out=o, in0=scy[:, 0:TS], in1=z, op=ALU.mult),
                        deps=(ev_yp, ev_sz, gev("scyp")))
                    sev(("scza", slot), ev_ypp)
                    ev_w = em.dma("gpsimd", ypV[d][:, ct, a0:a0 + TS], ys, "scypw",
                                  deps=(ev_ypp,))
                    sev("scyp", ev_w)
                    sev(("yp", d, j, ct), ev_w)

        def p89_mm(j):
            t0 = j * TC
            ev_m = None
            for di, d in enumerate("fb"):
                ybuf = e_yf if d == "f" else e_yb
                for ct in range(NCT):
                    slot = ct % 2
                    ev_yl = em.dma("sync", ybuf[:, slot * TC:slot * TC + TC],
                                   ypV[d][:, ct, t0:t0 + TC], f"ey{d}{slot}",
                                   deps=(gev(("yp", d, j, ct)), gev((d + "yb", slot))))
                    wslot = (di * NCT + ct) % 2
                    ev_wl = em.dma("sync", e_we[:, wslot * 1024:(wslot + 1) * 1024],
                                   WeV[d][:, ct, :], f"ewe{wslot}",
                                   deps=(we_done[d], gev(("e_we", wslot))))
                    for m in range(NKD):
                        s0 = (di == 0 and ct == 0)
                        s1 = (di == 1 and ct == NCT - 1)
                        dps = [ev_yl, ev_wl]
                        if s0:
                            dps += [bank_free[b] for b in range(NKD)]
                        ev_m = em.op("tensor",
                                     (lambda e, mm_=m, sl=slot, ws=wslot, st0=s0, st1=s1,
                                      yb=ybuf:
                                      e.matmul(ps[mm_][:, 0:TC],
                                               e_we[:, ws * 1024 + mm_ * 128:
                                                    ws * 1024 + (mm_ + 1) * 128],
                                               yb[:, sl * TC:sl * TC + TC],
                                               start=st0, stop=st1)),
                                     inc=("tv" if (s1 or m == NKD - 1) else None),
                                     deps=tuple(dps))
                    sev((d + "yb", slot), ev_m)
                    sev(("e_we", wslot), ev_m)
            ev_x0 = em.dma("sync", p1x.ap().rearrange("p (kt t) -> p kt t", t=TC),
                           xTv[:, :, t0:t0 + TC], "p1x", deps=(gev("p1x_free"),))
            sev(("p89mm", j), (ev_m, ev_x0))
            sev(("p89x0", j), ev_x0)
            sev(("p89close", j), ev_m)

        def p89_epi(j):
            t0 = j * TC
            ev_m = gev(("p89close", j))
            ev_x0 = gev(("p89x0", j))
            ev_at = []
            for m in range(NKD):
                evn = em.dma("sync", e_xn1[:, (m % 2) * TC:(m % 2) * TC + TC],
                             xn1v[:, m, t0:t0 + TC], f"exn{m % 2}",
                             deps=(gev(("xn1", j)), gev(("e_xn1", m % 2))))
                ev_at.append(vop(lambda e, mm_=m: e.scalar_tensor_tensor(
                    out=e_at[:, mm_ * TC:(mm_ + 1) * TC], in0=ps[mm_][:, 0:TC],
                    scalar=projb[:, mm_:mm_ + 1],
                    in1=e_xn1[:, (mm_ % 2) * TC:(mm_ % 2) * TC + TC],
                    op0=ALU.add, op1=ALU.add),
                    deps=(ev_m, evn, gev("e_at_free"), evc)))
                sev(("e_xn1", m % 2), ev_at[m])
                bank_used(m, ev_at[m])
            ers2, enm2 = stats(e_at, TC, deps=tuple(ev_at))
            rs3 = bcb[:, 0:TC].unsqueeze(1).broadcast_to([128, NKD, TC])
            nm3 = bcb[:, TC:2 * TC].unsqueeze(1).broadcast_to([128, NKD, TC])
            at3 = e_at.ap().rearrange("p (m t) -> p m t", m=NKD)
            x13 = e_x1.ap().rearrange("p (m t) -> p m t", m=NKD)
            x33 = e_x3.ap().rearrange("p (m t) -> p m t", m=NKD)
            e1 = vop(lambda e: e.tensor_tensor(out=x13, in0=at3, in1=rs3,
                                               op=ALU.mult),
                     deps=(ers2, gev("e_x1_free")) + tuple(ev_at))
            e2 = vop(lambda e: e.tensor_tensor(out=x13, in0=x13, in1=nm3,
                                               op=ALU.add), deps=(enm2, e1))
            e3 = vop(lambda e: e.tensor_tensor(out=e_x1[:], in0=e_x1[:],
                                               in1=p1x[:], op=ALU.add),
                     deps=(e2, ev_x0))
            ev_x1 = [e3] * NKD
            sev("e_at_free", e3)
            sev("p1x_free", e3)
            sev("bcb_free", e3)
            ers3, enm3 = stats(e_x1, TC, deps=(e3,), scl=QSCL)
            e1 = vop(lambda e: e.tensor_tensor(out=x33, in0=x13, in1=rs3,
                                               op=ALU.mult),
                     deps=(ers3, e3, gev("e_x3_free")))
            e2 = vop(lambda e: e.tensor_tensor(out=x33, in0=x33, in1=nm3,
                                               op=ALU.add), deps=(enm3, e1))
            ev_x3 = [e2] * NKD
            sev("bcb_free", e2)

            all_dn = []
            ev_hp = {}
            ev_g = None
            for m in range(NKH):
                ws = m % 2
                srcg = wgT.rearrange("(kt p) h -> p kt h", p=128)[
                    :, :, m * 128:(m + 1) * 128]
                srcu = wuT.rearrange("(kt p) h -> p kt h", p=128)[
                    :, :, m * 128:(m + 1) * 128]
                ev_wg_ = em.dma("gpsimd",
                                e_wg[:, ws * 1024:(ws + 1) * 1024]
                                .rearrange("p (kt c) -> p kt c", c=128),
                                srcg, f"ewg{ws}", deps=(gev(("e_wg", ws)),))
                ev_wu_ = em.dma("gpsimd",
                                e_wu[:, ws * 1024:(ws + 1) * 1024]
                                .rearrange("p (kt c) -> p kt c", c=128),
                                srcu, f"ewu{ws}", deps=(gev(("e_wu", ws)),))
                pgb = 0 + (m % 2)
                pub = 2 + (m % 2)
                pg = ps[pgb][:, 0:TC]
                pu = ps[pub][:, 0:TC]
                lg = [(pg, e_wg[:, ws * 1024 + kt * 128:ws * 1024 + (kt + 1) * 128],
                       e_x3[:, kt * TC:(kt + 1) * TC]) for kt in range(NKD)]
                lu = [(pu, e_wu[:, ws * 1024 + kt * 128:ws * 1024 + (kt + 1) * 128],
                       e_x3[:, kt * TC:(kt + 1) * TC]) for kt in range(NKD)]
                ev_g = mmg(lg, [pgb], deps=(ev_wg_, ev_x3[-1]))
                ev_u_ = mmg(lu, [pub], deps=(ev_wu_,))
                sev(("e_wg", ws), ev_g)
                sev(("e_wu", ws), ev_u_)
                gsl = e_gt[:, (m % 2) * TC:(m % 2) * TC + TC]
                ev_si = sop(lambda e, o=gsl, i=pg: e.activation(out=o, in_=i,
                                                               func=ACT.Silu),
                            deps=(ev_g, gev(("e_gt", m % 2))))
                bank_used(pgb, ev_si)
                ev_h = vop(lambda e, mm_=m, g=gsl, u=pu: e.scalar_tensor_tensor(
                    out=e_hp[:, mm_ * TC:(mm_ + 1) * TC], in0=g, scalar=QSCL,
                    in1=u, op0=ALU.mult, op1=ALU.mult),
                    deps=(ev_si, ev_u_, gev(("e_hp", m))))
                bank_used(pub, ev_h)
                sev(("e_gt", m % 2), ev_h)
                ev_hp[m] = ev_h
            ev_dn = {}
            for sw in range(2):
                ev = None
                for m in range(NKH):
                    ws = m % 2
                    src = wdT.rearrange("(m p) o -> m p o", p=128)[
                        m, :, sw * 512:(sw + 1) * 512]
                    ev_wd_ = em.dma("gpsimd", e_wd[:, ws * 512:(ws + 1) * 512],
                                    src, f"ewd{ws}", deps=(gev(("e_wd", ws)),))
                    for m2 in range(4):
                        g = sw * 4 + m2
                        bk = 4 + m2
                        dps = [ev_wd_, ev_hp[m]]
                        if m == 0:
                            dps.append(bank_free[bk])
                        ev = em.op("tensor",
                                   (lambda e, b=bk, mm_=m, ws_=ws, c2=m2,
                                    s0=(m == 0), s1=(m == NKH - 1):
                                    e.matmul(ps[b][:, 0:TC],
                                             e_wd[:, ws_ * 512 + c2 * 128:
                                                  ws_ * 512 + c2 * 128 + 128],
                                             e_hp[:, mm_ * TC:(mm_ + 1) * TC],
                                             start=s0, stop=s1)),
                                   inc=("tv" if (m == NKH - 1 or m2 == 3) else None),
                                   deps=tuple(dps))
                        if m == NKH - 1:
                            ev_dn[g] = ev
                    sev(("e_wd", ws), ev)
                for m2 in range(4):
                    g = sw * 4 + m2
                    ev_e = sop(lambda e, gg=g, b=4 + m2: e.copy(
                        out=e_at[:, gg * TC:(gg + 1) * TC],
                        in_=ps[b][:, 0:TC]),
                        deps=(ev_dn[g], gev("e_ffn_free")))
                    bank_used(4 + m2, ev_e)
                    ev_dn[g] = ev_e
                    all_dn.append(ev_e)
            for m in range(NKH):
                sev(("e_hp", m), ev_dn[NKD - 1])
            ers4, enm4 = stats(e_at, TC, deps=tuple(all_dn))
            e1 = vop(lambda e: e.tensor_tensor(out=x33, in0=at3, in1=rs3,
                                               op=ALU.mult),
                     deps=(ers4, gev("x3_out_free"), ev_g))
            e2 = vop(lambda e: e.tensor_tensor(out=x33, in0=x33, in1=nm3,
                                               op=ALU.add), deps=(enm4, e1))
            e3 = vop(lambda e: e.tensor_tensor(out=e_x3[:], in0=e_x3[:],
                                               in1=e_x1[:], op=ALU.add),
                     deps=(e2, ev_x1[-1]))
            ev_o = [e3] * NKD
            sev("e_ffn_free", ev_o[-1])
            sev("e_x1_free", ev_o[-1])
            sev("e_at_free", ev_o[-1])
            sev("bcb_free", ev_o[-1])
            ev_out = em.dma("gpsimd", yTv[:, :, t0:t0 + TC],
                            e_x3.ap().rearrange("p (kt t) -> p kt t", t=TC), "do",
                            deps=(ev_o[-1],))
            sev("x3_out_free", ev_out)
            sev("e_x3_free", ev_out)

        for j in range(NCH):
            in_proj_chunk("f", True, j)
            xp_dtp_chunk("f", j)
            if j > 0:
                scan_chunk("f", True, j - 1)
        scan_chunk("f", True, NCH - 1)
        for j in range(NCH - 1, -1, -1):
            if j + 2 < NCH:
                p89_epi(j + 2)
            in_proj_chunk("b", False, j)
            xp_dtp_chunk("b", j)
            if j + 1 < NCH:
                scan_chunk("b", False, j + 1)
                p89_mm(j + 1)
        scan_chunk("b", False, 0)
        p89_epi(1)
        p89_mm(0)
        p89_epi(0)

        em.op("gpsimd", lambda e: e.engine_nop(),
              deps=(("do", em.cnt.get("do", 0)),))

        sems = {}
        for name in em.cnt:
            sems[name] = stack.enter_context(nc.semaphore(name))
        blk = stack.enter_context(nc.Block())
        em.replay(blk, sems)
    lower_extended_insts(nc)
    return nc


def _col_form(v, nt):
    return np.ascontiguousarray(np.asarray(v, np.float32).reshape(nt, 128).T)


def _prep_core_inputs(xb, ins):
    m = {}
    m["xT"] = np.ascontiguousarray(np.asarray(xb, np.float32).T)
    for d, sfx in (("f", "_f"), ("b", "_b")):
        m[f"inwT_{d}"] = np.ascontiguousarray(np.asarray(ins[f"L0_attn_in_w{sfx}"], np.float32).T)
        m[f"outw_{d}"] = np.ascontiguousarray(np.asarray(ins[f"L0_attn_out_w{sfx}"], np.float32))
        cwv = np.asarray(ins[f"L0_attn_conv_w{sfx}"], np.float32).reshape(DI, 4)
        m[f"cw_{d}"] = np.ascontiguousarray(
            cwv.reshape(NCT, 128, 4).transpose(1, 0, 2).reshape(128, NCT * 4))
        m[f"cb_{d}"] = _col_form(ins[f"L0_attn_conv_b{sfx}"], NCT)
        xpv = np.asarray(ins[f"L0_attn_xp_w{sfx}"], np.float32).T
        m[f"xpw_{d}"] = np.ascontiguousarray(
            xpv.reshape(NCT, 128, 96).transpose(1, 0, 2).reshape(128, NCT * 96))
        m[f"dtpw_{d}"] = np.ascontiguousarray(np.asarray(ins[f"L0_attn_dtp_w{sfx}"], np.float32).T)
        m[f"dtpb_{d}"] = _col_form(ins[f"L0_attn_dtp_b{sfx}"], NCT)
        av = np.asarray(ins[f"L0_attn_A_log{sfx}"], np.float32)
        m[f"A_{d}"] = np.ascontiguousarray(
            av.reshape(NCT, 128, NS).transpose(1, 0, 2).reshape(128, NCT * NS))
        m[f"Dsk_{d}"] = _col_form(ins[f"L0_attn_D{sfx}"], NCT)
    m["gamR"] = _col_form(ins["L0_attn_ln_gamma"], NKD)
    m["betR"] = _col_form(ins["L0_attn_ln_beta"], NKD)
    m["projT"] = np.ascontiguousarray(np.asarray(ins["L0_attn_proj_w"], np.float32).T)
    m["projbR"] = _col_form(ins["L0_attn_proj_b"], NKD)
    m["wgT"] = np.ascontiguousarray(np.asarray(ins["L0_ffn_w_gate"], np.int32).T)
    m["wuT"] = np.ascontiguousarray(np.asarray(ins["L0_ffn_w_up"], np.int32).T)
    m["wdT"] = np.ascontiguousarray(np.asarray(ins["L0_ffn_w_down"], np.int32).T)
    return m


def kernel(**inputs):
    x = np.asarray(inputs["L0_attn_x"])
    B, L, _ = x.shape
    nc = build_nc(L=L)
    in_maps = [_prep_core_inputs(x[b], inputs) for b in range(B)]
    res = run_bass_kernel_spmd(nc, in_maps, core_ids=list(range(B)))
    out = np.stack([np.asarray(r["yT"]).T for r in res.results], axis=0)
    return np.ascontiguousarray(out.astype(np.float32))


# version 20 (final)
# speedup vs baseline: 1.0187x; 1.0180x over previous
import numpy as np
import concourse.bass as bass
import concourse.mybir as mybir
from concourse.bass_utils import run_bass_kernel_spmd
from concourse.bass_isa import ReduceOp
from concourse import library_config
from concourse.library_overlay import lower_extended_insts
from contextlib import ExitStack

F32 = mybir.dt.float32
I32 = mybir.dt.int32
ALU = mybir.AluOpType
ACT = mybir.ActivationFunctionType
AX = mybir.AxisListType

D = 1024
DI = 2048
NS = 16
DTR = 64
HID = 3072
NKD = D // 128
NCT = DI // 128
NKH = HID // 128
EPS = 1e-5
QSCL = float(np.float32(1.0 / 127.0))

ENGINES = ("sync", "tensor", "vector", "scalar", "gpsimd")


class EM:
    def __init__(self):
        self.q = {e: [] for e in ENGINES}
        self.cnt = {}
        self.mark = {}

    def op(self, eng, fn, inc=None, deps=(), amt=1):
        flat = []
        for dp in deps:
            if dp is None:
                continue
            if isinstance(dp, (list, tuple)) and dp and isinstance(dp[0], (list, tuple)):
                flat.extend(x for x in dp if x is not None)
            else:
                flat.append(dp)
        for dp in flat:
            s, v = dp
            if self.mark.get((eng, s), 0) < v:
                self.q[eng].append(("w", s, v))
                self.mark[(eng, s)] = v
        ev = None
        if inc is not None:
            self.cnt[inc] = self.cnt.get(inc, 0) + amt
            ev = (inc, self.cnt[inc])
        self.q[eng].append(("o", fn, inc, amt))
        return ev

    def dma(self, eng, out, in_, sem, deps=()):
        prior = self.cnt.get(sem, 0)
        return self.op(eng, lambda e, o=out, i=in_: e.dma_start(out=o, in_=i),
                       inc=sem, deps=deps + ((sem, prior),), amt=16)

    def replay(self, block, sems):
        for name in ENGINES:
            section = getattr(block, name)

            def make(qname):
                def body(eng):
                    for item in self.q[qname]:
                        if item[0] == "w":
                            eng.wait_ge(sems[item[1]], item[2])
                        else:
                            inst = item[1](eng)
                            if item[2] is not None:
                                inst.then_inc(sems[item[2]], item[3])
                return body

            section(make(name))


def build_nc(L=4096, taps=False):
    TC = 256
    NCH = L // TC
    TS = TC
    TC9 = 128

    nc = bass.Bass("TRN2", target_bir_lowering=False, debug=False)

    dram = lambda n, s, dt=F32, k="ExternalInput": nc.dram_tensor(n, s, dt, kind=k).ap()
    xT = dram("xT", [D, L])
    inwT = {d: dram(f"inwT_{d}", [D, 2 * DI]) for d in "fb"}
    outw = {d: dram(f"outw_{d}", [D, DI]) for d in "fb"}
    cwR = {d: dram(f"cw_{d}", [128, NCT * 4]) for d in "fb"}
    cbR = {d: dram(f"cb_{d}", [128, NCT]) for d in "fb"}
    xpwR = {d: dram(f"xpw_{d}", [128, NCT * 96]) for d in "fb"}
    dtpwR = {d: dram(f"dtpw_{d}", [DTR, DI]) for d in "fb"}
    dtpbR = {d: dram(f"dtpb_{d}", [128, NCT]) for d in "fb"}
    AR = {d: dram(f"A_{d}", [128, NCT * NS]) for d in "fb"}
    DskR = {d: dram(f"Dsk_{d}", [128, NCT]) for d in "fb"}
    gamR = dram("gamR", [128, NKD])
    betR = dram("betR", [128, NKD])
    projTD = dram("projT", [2 * D, D])
    projbR = dram("projbR", [128, NKD])
    wgT = dram("wgT", [D, HID], I32)
    wuT = dram("wuT", [D, HID], I32)
    wdT = dram("wdT", [HID, D], I32)

    ks = "ExternalOutput" if taps else "Internal"
    WeT = {d: dram(f"WeT_{d}", [DI, D], F32, ks) for d in "fb"}
    xn1S = dram("xn1S", [D, L], F32, ks)
    xn2S = dram("xn2S", [D, L], F32, ks)
    uS = {d: dram(f"uS_{d}", [DI, L], F32, ks) for d in "fb"}
    zaS = {d: dram(f"zaS_{d}", [DI, L], F32, ks) for d in "fb"}
    dlS = {d: dram(f"dlS_{d}", [DI, L], F32, ks) for d in "fb"}
    bcSD = {d: dram(f"bcS_{d}", [32, L], F32, ks) for d in "fb"}
    ypS = {d: dram(f"ypS_{d}", [DI, L], F32, ks) for d in "fb"}
    yT = dram("yT", [D, L], F32, "ExternalOutput")

    def dview(t, p=128):
        return t.rearrange("(a p) t -> p a t", p=p)
    xTv = dview(xT)
    xn1v = dview(xn1S)
    xn2v = dview(xn2S)
    yTv = dview(yT)
    uV = {d: dview(uS[d]) for d in "fb"}
    zaV = {d: dview(zaS[d]) for d in "fb"}
    dlV = {d: dview(dlS[d]) for d in "fb"}
    ypV = {d: dview(ypS[d]) for d in "fb"}
    WeV = {d: dview(WeT[d]) for d in "fb"}

    em = EM()
    st = {}
    sev = st.__setitem__
    gev = st.get

    stack = ExitStack()
    sb = lambda name, shape, dt=F32: stack.enter_context(nc.sbuf_tensor(name, shape, dt))
    psmk = lambda name, shape: stack.enter_context(nc.psum_tensor(name, shape, F32))

    with stack:
        cw = {d: sb(f"cw{d}", [128, NCT * 4]) for d in "fb"}
        cb = {d: sb(f"cb{d}", [128, NCT]) for d in "fb"}
        dtpb = {d: sb(f"dtpb{d}", [128, NCT]) for d in "fb"}
        As = {d: sb(f"As{d}", [128, NCT * NS]) for d in "fb"}
        Dsk = {d: sb(f"Dsk{d}", [128, NCT]) for d in "fb"}
        projb = sb("projb", [128, NKD])
        gam = sb("gam", [128, NKD])
        bet = sb("bet", [128, NKD])
        hst = {d: sb(f"hst{d}", [128, NCT * NS]) for d in "fb"}
        halo = {d: sb(f"halo{d}", [128, NCT * 4]) for d in "fb"}

        p1x = sb("p1x", [128, NKD * TC])
        sqb = sb("sqb", [128, 2 * TC])
        bcb = sb("bcb", [128, 2 * TC])

        ipslab = sb("ipslab", [128, 2 * 1024])
        ipxn2 = sb("ipxn2", [128, NKD * TC])
        ipring = sb("ipring", [128, 2 * (TC + 8)])
        ipcvt = sb("ipcvt", [128, TC])
        ipout = sb("ipout", [128, 2 * TC])

        p4u = sb("p4u", [128, 2 * TC])
        p4dt = sb("p4dt", [64, TC])
        p4z0 = sb("p4z0", [128, 2 * TC])
        p4sp = sb("p4sp", [128, TC])
        p4xpw = sb("p4xpw", [128, 2 * 96])
        p4bc = sb("p4bc", [128, TC])
        p4dw = sb("p4dw", [64, 2 * 128])

        scdl = sb("scdl", [128, 2 * TS])
        scu = sb("scu", [128, 2 * TS])
        scza = sb("scza", [128, 2 * TS])
        scdA = sb("scdA", [128, (NS // 2) * TS])
        scbh = sb("scbh", [128, NS * TS])
        scB = sb("scB", [128, NS * TS])
        scC = sb("scC", [128, NS * TS])
        scdu = sb("scdu", [128, TS])
        scy = sb("scy", [128, TS])
        scyp = sb("scyp", [128, 2 * TS])

        e_yf = sb("e_yf", [128, 4 * TC])
        e_yb = sb("e_yb", [128, 4 * TC])
        e_we = sb("e_we", [128, 2 * 1024])
        e_xn1 = sb("e_xn1", [128, 2 * TC])
        e_at = sb("e_at", [128, NKD * TC])
        e_x1 = sb("e_x1", [128, NKD * TC])
        e_x3 = sb("e_x3", [128, NKD * TC])
        e_hp = sb("e_hp", [128, NKH * TC])
        e_gt = sb("e_gt", [128, 2 * TC])
        e_wg = sb("e_wg", [128, 2 * 1024])
        e_wu = sb("e_wu", [128, 2 * 1024])
        e_wd = sb("e_wd", [128, 2 * 512])

        ps = [psmk(f"ps{i}", [128, 512]) for i in range(8)]

        blk_holder = []

        def prj_slice(kt, n2):
            buf = (ipxn2, p1x, e_at, e_x1)[kt // 2]
            off = (kt % 2) * 1024 + n2 * 512
            return buf[:, off:off + 512]

        def vop(fn, deps=(), inc="vv"):
            return em.op("vector", fn, inc=inc, deps=deps)

        def sop(fn, deps=(), inc="sv"):
            return em.op("scalar", fn, inc=inc, deps=deps)

        bank_free = {i: None for i in range(8)}

        def mmg(lst, banks, deps=(), inc="tv"):
            bdeps = tuple(bank_free[b] for b in banks)
            ev = None
            n = len(lst)
            for i, (o, l, r) in enumerate(lst):
                ev = em.op("tensor",
                           (lambda e, oo=o, ll=l, rr=r, s0=(i == 0), s1=(i == n - 1):
                            e.matmul(oo, ll, rr, start=s0, stop=s1)),
                           inc=(inc if i == n - 1 else None),
                           deps=(deps + bdeps) if i == 0 else ())
            return ev

        def bank_used(b, ev):
            bank_free[b] = ev

        def gop(fn, deps=(), inc="gv"):
            return em.op("gpsimd", fn, inc=inc, deps=deps)

        def stats(xbuf, T, deps, scl=1.0):
            assert T == TC
            dep_bcb = (gev("bcb_free"),)
            rsg = bcb[:, 0:T]
            xsm = bcb[:, TC:TC + T]
            x3 = xbuf[:].rearrange("p (a t) -> p t a", a=NKD)
            e_xs = vop(lambda e: e.tensor_reduce(out=xsm, in_=x3, axis=AX.X,
                                                 op=ALU.add), deps=deps + dep_bcb)
            e_arx = gop(lambda e: e.partition_all_reduce(xsm, xsm, 128,
                                                         ReduceOp.add),
                        deps=(e_xs,))
            e_sq0 = sop(lambda e: e.activation(out=rsg, in_=xbuf[:, 0:T],
                                               func=ACT.Square),
                        deps=deps + dep_bcb)
            e_add = e_sq0
            for kt in range(1, NKD):
                sl = sqb[:, (kt % 2) * TC:(kt % 2) * TC + T]
                e_sq = sop(lambda e, k=kt, o=sl: e.activation(
                    out=o, in_=xbuf[:, k * TC:k * TC + T], func=ACT.Square),
                    deps=deps + (gev(("sqf", kt % 2)),))
                e_add = vop(lambda e, i=sl: e.tensor_tensor(
                    out=rsg, in0=rsg, in1=i, op=ALU.add), deps=(e_sq, e_add))
                sev(("sqf", kt % 2), e_add)
            e_arq = gop(lambda e: e.partition_all_reduce(rsg, rsg, 128,
                                                         ReduceOp.add),
                        deps=(e_add,))
            t0 = sqb[:, 0:TC]
            t1 = sqb[:, TC:2 * TC]
            e_t = vop(lambda e: e.scalar_tensor_tensor(
                out=t0, in0=xsm, scalar=1.0 / D, in1=xsm,
                op0=ALU.mult, op1=ALU.mult),
                deps=(e_arx, gev(("sqf", 0))))
            e_v = vop(lambda e: e.tensor_tensor(out=t0, in0=rsg, in1=t0,
                                                op=ALU.subtract),
                      deps=(e_arq, e_t))
            e_v = vop(lambda e: e.tensor_scalar(out=t0, in0=t0, scalar1=1.0 / D,
                                                scalar2=EPS, op0=ALU.mult,
                                                op1=ALU.add), deps=(e_v,))
            e_l = sop(lambda e: e.activation(out=t1, in_=t0, func=ACT.Ln),
                      deps=(e_v, gev(("sqf", 1))))
            e_r0 = sop(lambda e: e.activation(out=rsg, in_=t1, func=ACT.Exp,
                                              scale=-0.5), deps=(e_l,))
            e_n = vop(lambda e: e.tensor_tensor(out=t1, in0=rsg, in1=rsg,
                                                op=ALU.mult), deps=(e_r0,))
            e_n2 = vop(lambda e: e.tensor_tensor(out=t1, in0=t1, in1=t0,
                                                 op=ALU.mult), deps=(e_n,))
            e_n3 = vop(lambda e: e.tensor_scalar(out=t1, in0=t1,
                                                 scalar1=-0.5 * scl,
                                                 scalar2=1.5 * scl,
                                                 op0=ALU.mult, op1=ALU.add),
                       deps=(e_n2,))
            ec1 = vop(lambda e: e.tensor_tensor(out=rsg, in0=rsg, in1=t1,
                                                op=ALU.mult), deps=(e_n3,))
            ec2 = vop(lambda e: e.scalar_tensor_tensor(
                out=xsm, in0=xsm, scalar=-1.0 / D, in1=rsg,
                op0=ALU.mult, op1=ALU.mult), deps=(ec1,))
            sev(("sqf", 0), e_n2)
            sev(("sqf", 1), ec1)
            return ec1, ec2

        em.op("gpsimd", lambda e: e.load_library(library_config.attn))
        ev_c = []
        for d in "fb":
            ev_c.append(em.dma("sync", cw[d][:], cwR[d], "cst"))
            ev_c.append(em.dma("sync", cb[d][:], cbR[d], "cst"))
            ev_c.append(em.dma("sync", dtpb[d][:], dtpbR[d], "cst"))
            ev_c.append(em.dma("sync", As[d][:], AR[d], "cst"))
            ev_c.append(em.dma("sync", Dsk[d][:], DskR[d], "cst"))
        ev_c.append(em.dma("sync", projb[:], projbR, "cst"))
        ev_c.append(em.dma("sync", gam[:], gamR, "cst"))
        ev_c.append(em.dma("sync", bet[:], betR, "cst"))
        evc = ev_c[-1]
        vop(lambda e: e.memset(hst["f"][:], 0.0))
        ev_hz = vop(lambda e: e.memset(hst["b"][:], 0.0))
        vop(lambda e: e.memset(halo["f"][:], 0.0))
        ev_halo = vop(lambda e: e.memset(halo["b"][:], 0.0))

        ev_As = {}
        for d in "fb":
            ea = sop(lambda e, dd=d: e.activation(out=As[dd][:], in_=As[dd][:],
                                                  func=ACT.Exp), deps=(evc,))
            ev_As[d] = vop(lambda e, dd=d: e.tensor_scalar_mul(
                out=As[dd][:], in0=As[dd][:], scalar1=-1.0), deps=(ea,))

        we_done = {}
        prjv = projTD.rearrange("(h kt p) o -> h p kt o", h=2, p=128)
        for di, d in enumerate("fb"):
            ev_p = []
            for kt in range(NKD):
                for n2 in range(2):
                    ev_p.append(em.dma("sync", prj_slice(kt, n2),
                                       prjv[di, :, kt, n2 * 512:(n2 + 1) * 512], "prj",
                                       deps=(gev("prj_readers"),)))
            ev_pl = ev_p[-1]
            for m in range(NCT):
                slot = m % 2
                src = outw[d].rearrange("(kt p) c -> p kt c", p=128)[
                    :, :, m * 128:(m + 1) * 128]
                ev_l = em.dma("sync",
                              ipslab[:, slot * 1024:(slot + 1) * 1024]
                              .rearrange("p (kt c) -> p kt c", c=128),
                              src, f"ipsl{slot}", deps=(gev(("p0slab", slot)),))
                ev_mm = None
                for n2 in range(2):
                    lst = [(ps[n2][:, 0:512],
                            ipslab[:, slot * 1024 + kt * 128:slot * 1024 + (kt + 1) * 128],
                            prj_slice(kt, n2)) for kt in range(NKD)]
                    ev_mm = mmg(lst, [n2], deps=(ev_l, ev_pl))
                    stg = e_hp[:, n2 * 512:(n2 + 1) * 512]
                    ev_cp = sop(lambda e, o=stg, i=ps[n2][:, 0:512]: e.copy(out=o, in_=i),
                                deps=(ev_mm, gev(("p0stg", n2))))
                    bank_used(n2, ev_cp)
                    dst = WeV[d][:, m, n2 * 512:(n2 + 1) * 512]
                    ev_wr = em.dma("scalar", dst, stg, f"p0st{n2}", deps=(ev_cp,))
                    sev(("p0stg", n2), ev_wr)
                    sev(("p0last", n2), ev_wr)
                    we_done[d] = ev_wr
                sev(("p0slab", slot), ev_mm)
            sev("prj_readers", ev_mm)
        for m_ in range(4):
            sev(("e_hp", m_), (gev(("p0last", 0)), gev(("p0last", 1))))
        for d in "fb":
            we_done[d] = (we_done[d], gev(("p0last", 0)), gev(("p0last", 1)))
        sev("prj_free", we_done["b"])

        sev("p1x_free", we_done["b"])
        sev("ipxn2_free", we_done["b"])
        sev("e_at_free", we_done["b"])
        sev("e_x1_free", we_done["b"])

        p1_order = []
        lo, hi = 0, NCH - 1
        while lo <= hi:
            p1_order.append(lo)
            lo += 1
            if lo <= hi:
                p1_order.append(hi)
                hi -= 1

        for j in p1_order:
            t0 = j * TC
            ev_x = em.dma("sync", p1x.ap().rearrange("p (kt t) -> p kt t", t=TC),
                          xTv[:, :, t0:t0 + TC], "p1x", deps=(gev("p1x_free"),))
            ers, enm = stats(p1x, TC, deps=(ev_x,))
            wlast = {}
            for kt in range(NKD):
                sx = p1x[:, kt * TC:(kt + 1) * TC]
                e1 = vop(lambda e, a=sx: e.tensor_tensor(out=a, in0=a, in1=bcb[:, 0:TC],
                                                         op=ALU.mult), deps=(ers,))
                e2 = vop(lambda e, a=sx: e.tensor_tensor(out=a, in0=a,
                                                         in1=bcb[:, TC:2 * TC],
                                                         op=ALU.add), deps=(enm, e1))
                ew1 = em.dma("gpsimd", xn1v[:, kt, t0:t0 + TC], sx, f"p1w{kt % 4}", deps=(e2,))
                e3 = vop(lambda e, a=sx, k=kt: e.scalar_tensor_tensor(
                    out=a, in0=a, scalar=gam[:, k:k + 1],
                    in1=bet[:, k:k + 1].broadcast_to([128, TC]),
                    op0=ALU.mult, op1=ALU.add), deps=(ew1, evc))
                ew2 = em.dma("gpsimd", xn2v[:, kt, t0:t0 + TC], sx, f"p1w{kt % 4}", deps=(e3,))
                wlast[kt % 4] = ew2
            comp = tuple(wlast.values())
            sev("p1x_free", comp)
            sev(("xn1", j), comp)
            sev(("xn2", j), comp)
            sev("bcb_free", e3)

        def in_proj_chunk(d, asc, j):
            if True:
                t0 = j * TC
                ev_xn = em.dma("sync", ipxn2.ap().rearrange("p (kt t) -> p kt t", t=TC),
                               xn2v[:, :, t0:t0 + TC], "ipxn2",
                               deps=(gev(("xn2", j)), gev("ipxn2_free")))
                ev_last = None
                for m in range(2 * NCT):
                    slot = m % 2
                    src = inwT[d].rearrange("(kt p) o -> p kt o", p=128)[
                        :, :, m * 128:(m + 1) * 128]
                    ev_l = em.dma("sync",
                                  ipslab[:, slot * 1024:(slot + 1) * 1024]
                                  .rearrange("p (kt c) -> p kt c", c=128),
                                  src, f"ipsl{slot}", deps=(gev(("ipslab", slot)),))
                    pm = ps[slot][:, 0:TC]
                    lst = [(pm, ipslab[:, slot * 1024 + kt * 128:slot * 1024 + (kt + 1) * 128],
                            ipxn2[:, kt * TC:(kt + 1) * TC]) for kt in range(NKD)]
                    ev_mm = mmg(lst, [slot], deps=(ev_l, ev_xn))
                    sev(("ipslab", slot), ev_mm)
                    ev_last = ev_mm
                    if m < NCT:
                        ct = m
                        ring = ipring[:, slot * (TC + 8):slot * (TC + 8) + TC + 8]
                        ev_ev = sop(lambda e, o=ring[:, 4:4 + TC], i=pm: e.copy(out=o, in_=i),
                                    deps=(ev_mm, gev(("ring_busy", slot))))
                        bank_used(slot, ev_ev)
                        cvt = ipcvt[:, 0:TC]
                        if asc:
                            ev_h = vop(lambda e, c=ct, rg=ring: e.tensor_copy(
                                out=rg[:, 1:4], in_=halo[d][:, c * 4:c * 4 + 3]),
                                deps=(gev(("halo", d, ct)), ev_halo,
                                      gev(("ring_busy", slot))))
                            e1 = vop(lambda e, c=ct, o=cvt, rg=ring: e.tensor_scalar(
                                out=o, in0=rg[:, 1:1 + TC],
                                scalar1=cw[d][:, c * 4:c * 4 + 1], scalar2=None,
                                op0=ALU.mult),
                                deps=(ev_ev, ev_h, gev("cvt_busy")))
                            for k in range(1, 4):
                                e1 = vop(lambda e, kk=k, c=ct, o=cvt, rg=ring:
                                         e.scalar_tensor_tensor(
                                             out=o, in0=rg[:, 1 + kk:1 + kk + TC],
                                             scalar=cw[d][:, c * 4 + kk:c * 4 + kk + 1],
                                             in1=o, op0=ALU.mult, op1=ALU.add), deps=(e1,))
                            ev_hs = vop(lambda e, c=ct, rg=ring: e.tensor_copy(
                                out=halo[d][:, c * 4:c * 4 + 3],
                                in_=rg[:, 1 + TC:4 + TC]), deps=(ev_ev,))
                        else:
                            ev_h = vop(lambda e, c=ct, rg=ring: e.tensor_copy(
                                out=rg[:, 4 + TC:4 + TC + 3],
                                in_=halo[d][:, c * 4:c * 4 + 3]),
                                deps=(gev(("halo", d, ct)), ev_halo,
                                      gev(("ring_busy", slot))))
                            e1 = vop(lambda e, c=ct, o=cvt, rg=ring: e.tensor_scalar(
                                out=o, in0=rg[:, 4:4 + TC],
                                scalar1=cw[d][:, c * 4 + 3:c * 4 + 4], scalar2=None,
                                op0=ALU.mult),
                                deps=(ev_ev, ev_h, gev("cvt_busy")))
                            for k in range(1, 4):
                                e1 = vop(lambda e, kk=k, c=ct, o=cvt, rg=ring:
                                         e.scalar_tensor_tensor(
                                             out=o, in0=rg[:, 4 + kk:4 + kk + TC],
                                             scalar=cw[d][:, c * 4 + 3 - kk:c * 4 + 4 - kk],
                                             in1=o, op0=ALU.mult, op1=ALU.add), deps=(e1,))
                            ev_hs = vop(lambda e, c=ct, rg=ring: e.tensor_copy(
                                out=halo[d][:, c * 4:c * 4 + 3], in_=rg[:, 4:7]),
                                deps=(ev_ev,))
                        sev(("halo", d, ct), ev_hs)
                        sev(("ring_busy", slot), e1)
                        uo = ipout[:, slot * TC:slot * TC + TC]
                        ev_u = sop(lambda e, c=ct, o=uo, i=cvt: e.activation(
                            out=o, in_=i, func=ACT.Silu, bias=cb[d][:, c:c + 1]),
                            deps=(e1, gev(("out_busy", slot))))
                        sev("cvt_busy", ev_u)
                        ev_w = em.dma("gpsimd", uV[d][:, ct, t0:t0 + TC], uo, f"ipo{slot}",
                                      deps=(ev_u,))
                        sev(("out_busy", slot), ev_w)
                        sev(("u", d, j, ct), ev_w)
                    else:
                        ct = m - NCT
                        zo = ipout[:, slot * TC:slot * TC + TC]
                        ev_z = sop(lambda e, o=zo, i=pm: e.activation(
                            out=o, in_=i, func=ACT.Silu),
                            deps=(ev_mm, gev(("out_busy", slot))))
                        bank_used(slot, ev_z)
                        ev_w = em.dma("gpsimd", zaV[d][:, ct, t0:t0 + TC], zo, f"ipo{slot}",
                                      deps=(ev_z,))
                        sev(("out_busy", slot), ev_w)
                        sev(("za", d, j, ct), ev_w)
                sev("ipxn2_free", ev_last)

        def xp_dtp_chunk(d, j):
            if True:
                t0 = j * TC
                ev_mm = None
                for ct in range(NCT):
                    slot = ct % 2
                    ev_l = em.dma("sync", p4u[:, slot * TC:slot * TC + TC],
                                  uV[d][:, ct, t0:t0 + TC], f"p4u{slot}",
                                  deps=(gev(("u", d, j, ct)), gev(("p4u", slot))))
                    ev_xw = em.dma("sync", p4xpw[:, slot * 96:slot * 96 + 96],
                                   xpwR[d][:, ct * 96:(ct + 1) * 96], f"p4xw{slot}",
                                   deps=(gev(("p4xpw", slot)),))
                    ev_mm = em.op("tensor",
                                  (lambda e, sl=slot, s0=(ct == 0), s1=(ct == NCT - 1):
                                   e.matmul(ps[2][0:96, 0:TC],
                                            p4xpw[:, sl * 96:sl * 96 + 96],
                                            p4u[:, sl * TC:sl * TC + TC],
                                            start=s0, stop=s1)),
                                  inc="tv",
                                  deps=((ev_l, ev_xw, bank_free[2]) if ct == 0
                                        else (ev_l, ev_xw)))
                    sev(("p4u", slot), ev_mm)
                    sev(("p4xpw", slot), ev_mm)
                ev_dt = sop(lambda e: e.copy(out=p4dt[:, 0:TC], in_=ps[2][0:DTR, 0:TC]),
                            deps=(ev_mm, gev("p4dt_free")))
                ev_bcc = sop(lambda e: e.copy(out=p4bc[DTR:96, 0:TC],
                                              in_=ps[2][DTR:96, 0:TC]),
                             deps=(ev_mm, gev("p4bc_free")))
                ev_bc = em.dma("scalar", bcSD[d][:, t0:t0 + TC], p4bc[DTR:96, 0:TC], "p4bc",
                               deps=(ev_bcc,))
                sev(("bc", d, j), ev_bc)
                sev("p4bc_free", ev_bc)
                bank_used(2, max(ev_bcc, ev_dt, key=lambda x: 0))
                ev_d = None
                for ct in range(NCT):
                    slot = ct % 2
                    ev_dw_ = em.dma("sync", p4dw[0:DTR, slot * 128:slot * 128 + 128],
                                    dtpwR[d][:, ct * 128:(ct + 1) * 128], f"p4dw{slot}",
                                    deps=(gev(("p4dw", slot)),))
                    pm = ps[3 + slot][:, 0:TC]
                    ev_d = mmg([(pm, p4dw[0:DTR, slot * 128:slot * 128 + 128],
                                 p4dt[:, 0:TC])], [3 + slot],
                               deps=(ev_dt, ev_dw_))
                    sev(("p4dw", slot), ev_d)
                    z0 = p4z0[:, slot * TC:slot * TC + TC]
                    e1 = vop(lambda e, o=z0, i=pm, c=ct: e.tensor_scalar_add(
                        out=o, in0=i, scalar1=dtpb[d][:, c:c + 1]),
                        deps=(ev_d, gev(("p4z0", slot))))
                    bank_used(3 + slot, e1)
                    e2 = sop(lambda e, i=z0: e.activation(
                        out=p4sp[:, 0:TC], in_=i, func=ACT.Abs), deps=(e1,))
                    e3 = sop(lambda e: e.activation(
                        out=p4sp[:, 0:TC], in_=p4sp[:, 0:TC], func=ACT.Exp, scale=-1.0),
                        deps=(e2,))
                    e4 = sop(lambda e: e.activation(
                        out=p4sp[:, 0:TC], in_=p4sp[:, 0:TC], func=ACT.Ln, bias=1.0),
                        deps=(e3,))
                    e5 = vop(lambda e, o=z0: e.scalar_tensor_tensor(
                        out=o, in0=o, scalar=0.0, in1=p4sp[:, 0:TC],
                        op0=ALU.max, op1=ALU.add), deps=(e4,))
                    ev_w = em.dma("gpsimd", dlV[d][:, ct, t0:t0 + TC], z0, f"p4z{slot}",
                                  deps=(e5,))
                    sev(("p4z0", slot), ev_w)
                    sev(("dl", d, j, ct), ev_w)
                sev("p4dt_free", ev_d)

        def scan_chunk(d, asc, j):
            rev = not asc
            if True:
                a0 = j * TC
                ev_B = [None] * NS
                ev_C = [None] * NS
                for n in range(NS):
                    bs = n % 4
                    srcB = bcSD[d][n:n + 1, a0:a0 + TS].partition_broadcast(128).squeeze(1)
                    ev_B[n] = em.dma("scalar", scB[:, n * TS:(n + 1) * TS], srcB, f"scB{bs}",
                                     deps=(gev(("bc", d, j)), gev(("scB", n))))
                    srcC = bcSD[d][NS + n:NS + n + 1, a0:a0 + TS]\
                        .partition_broadcast(128).squeeze(1)
                    ev_C[n] = em.dma("scalar", scC[:, n * TS:(n + 1) * TS], srcC, f"scC{bs}",
                                     deps=(gev(("bc", d, j)), gev(("scC", n))))
                for ct in range(NCT):
                    slot = ct % 2
                    ev_dl = em.dma("sync", scdl[:, slot * TS:slot * TS + TS],
                                   dlV[d][:, ct, a0:a0 + TS], f"scdl{slot}",
                                   deps=(gev(("dl", d, j, ct)), gev(("scdl", slot))))
                    ev_su = em.dma("sync", scu[:, slot * TS:slot * TS + TS],
                                   uV[d][:, ct, a0:a0 + TS], f"scu{slot}",
                                   deps=(gev(("u", d, j, ct)), gev(("scu", slot))))
                    ev_sz = em.dma("sync", scza[:, slot * TS:slot * TS + TS],
                                   zaV[d][:, ct, a0:a0 + TS], f"scza{slot}",
                                   deps=(gev(("za", d, j, ct)), gev(("scza", slot))))
                    dl = scdl[:, slot * TS:slot * TS + TS]
                    uu = scu[:, slot * TS:slot * TS + TS]
                    za = scza[:, slot * TS:slot * TS + TS]
                    ev_da = [None] * NS

                    def emit_da(n, extra=()):
                        sl8 = n % 8
                        ev_da[n] = sop(lambda e, nn=n, s8=sl8, c=ct, i=dl:
                                       e.activation(
                                           out=scdA[:, s8 * TS:(s8 + 1) * TS],
                                           in_=i, func=ACT.Exp,
                                           scale=As[d][:, c * NS + nn:
                                                       c * NS + nn + 1]),
                                       deps=((ev_dl, ev_As[d]) if n == 0 else ())
                                            + (gev(("scdA", sl8)),) + extra)

                    for n in range(8):
                        emit_da(n)
                    ev_du = vop(lambda e, i=dl, u=uu: e.tensor_tensor(
                        out=scdu[:, 0:TS], in0=i, in1=u, op=ALU.mult),
                        deps=(ev_dl, ev_su, gev("scdu_free")))
                    bh3 = scbh.ap().rearrange("p (n t) -> p n t", n=NS)
                    du3 = scdu[:, 0:TS].unsqueeze(1).broadcast_to([128, NS, TS])
                    sB3 = scB.ap().rearrange("p (n t) -> p n t", n=NS)
                    ev_b = vop(lambda e: e.tensor_tensor(
                        out=bh3, in0=du3, in1=sB3, op=ALU.mult),
                        deps=(ev_du, gev("scbh_free")) + tuple(ev_B))
                    if ct == NCT - 1:
                        for n in range(NS):
                            sev(("scB", n), ev_b)
                    ev_scan = [None] * NS
                    for n in range(NS):
                        a_sl = scdA[:, (n % 8) * TS:(n % 8 + 1) * TS]
                        b_sl = scbh[:, n * TS:(n + 1) * TS]
                        if rev:
                            a_sl = a_sl[:, ::-1]
                            b_sl = b_sl[:, ::-1]
                        ev_scan[n] = vop(lambda e, a=a_sl, b=b_sl, c=ct, nn=n:
                                         e.tensor_tensor_scan(
                                             out=b, data0=a, data1=b,
                                             initial=hst[d][:, c * NS + nn:c * NS + nn + 1],
                                             op0=ALU.mult, op1=ALU.add),
                                         deps=(ev_b, ev_da[n], ev_hz))
                        sev(("scdA", n % 8), ev_scan[n])
                        if n < 8:
                            emit_da(n + 8, extra=(ev_da[n],))
                    bcol = 0 if rev else TS - 1
                    ev_hst = vop(lambda e, c=ct: e.tensor_copy(
                        out=hst[d][:, c * NS:(c + 1) * NS],
                        in_=scbh.ap().rearrange("p (n t) -> p n t", n=NS)[:, :, bcol]),
                        deps=(ev_scan[-1],))
                    sC3 = scC.ap().rearrange("p (n t) -> p n t", n=NS)
                    ev_hcb = vop(lambda e: e.tensor_tensor(
                        out=bh3, in0=bh3, in1=sC3, op=ALU.mult),
                        deps=(ev_scan[-1], ev_hst) + tuple(ev_C))
                    ev_hc = [ev_hcb] * NS
                    if ct == NCT - 1:
                        for n in range(NS):
                            sev(("scC", n), ev_hcb)
                    hc_v = scbh.ap().rearrange("p (n t) -> p t n", n=NS)
                    ev_y = vop(lambda e: e.tensor_reduce(out=scy[:, 0:TS], in_=hc_v,
                                                         axis=AX.X, op=ALU.add),
                               deps=(ev_hc[-1],))
                    sev("scbh_free", ev_y)
                    ev_yp = vop(lambda e, c=ct, u=uu: e.scalar_tensor_tensor(
                        out=scy[:, 0:TS], in0=u, scalar=Dsk[d][:, c:c + 1],
                        in1=scy[:, 0:TS], op0=ALU.mult, op1=ALU.add),
                        deps=(ev_y, ev_su, evc))
                    sev("scdu_free", ev_yp)
                    sev(("scdl", slot), ev_yp)
                    sev(("scu", slot), ev_yp)
                    ys = scyp[:, (ct % 2) * TS:(ct % 2) * TS + TS]
                    ev_ypp = vop(lambda e, o=ys, z=za: e.tensor_tensor(
                        out=o, in0=scy[:, 0:TS], in1=z, op=ALU.mult),
                        deps=(ev_yp, ev_sz, gev("scyp")))
                    sev(("scza", slot), ev_ypp)
                    if ct % 2 == 1:
                        ev_w = em.dma("gpsimd",
                                      ypV[d][:, ct - 1:ct + 1, a0:a0 + TS],
                                      scyp.ap().rearrange("p (c t) -> p c t", c=2),
                                      "scypw", deps=(ev_ypp,))
                        sev("scyp", ev_w)
                        sev(("yp", d, j, ct - 1), ev_w)
                        sev(("yp", d, j, ct), ev_w)

        def p89_mm(j):
            t0 = j * TC
            ev_m = None
            for di, d in enumerate("fb"):
                ybuf = e_yf if d == "f" else e_yb
                for ct0 in range(0, NCT, 2):
                    pslot = (ct0 // 2) % 2
                    ev_yl = em.dma("sync",
                                   ybuf[:, pslot * 2 * TC:(pslot + 1) * 2 * TC]
                                   .rearrange("p (c t) -> p c t", c=2),
                                   ypV[d][:, ct0:ct0 + 2, t0:t0 + TC], f"ey{d}{pslot}",
                                   deps=(gev(("yp", d, j, ct0)),
                                         gev(("yp", d, j, ct0 + 1)),
                                         gev((d + "yb", pslot))))
                    for sub in range(2):
                        ct = ct0 + sub
                        slot = pslot * 2 + sub
                        wslot = (di * NCT + ct) % 2
                        ev_wl = em.dma("sync", e_we[:, wslot * 1024:(wslot + 1) * 1024],
                                       WeV[d][:, ct, :], f"ewe{wslot}",
                                       deps=(we_done[d], gev(("e_we", wslot))))
                        for m in range(NKD):
                            s0 = (di == 0 and ct == 0)
                            s1 = (di == 1 and ct == NCT - 1)
                            dps = [ev_yl, ev_wl]
                            if s0:
                                dps += [bank_free[b] for b in range(NKD)]
                            ev_m = em.op("tensor",
                                         (lambda e, mm_=m, sl=slot, ws=wslot, st0=s0,
                                          st1=s1, yb=ybuf:
                                          e.matmul(ps[mm_][:, 0:TC],
                                                   e_we[:, ws * 1024 + mm_ * 128:
                                                        ws * 1024 + (mm_ + 1) * 128],
                                                   yb[:, sl * TC:sl * TC + TC],
                                                   start=st0, stop=st1)),
                                         inc=("tv" if (s1 or m == NKD - 1) else None),
                                         deps=tuple(dps))
                        sev(("e_we", wslot), ev_m)
                    sev((d + "yb", pslot), ev_m)
            ev_x0 = em.dma("sync", p1x.ap().rearrange("p (kt t) -> p kt t", t=TC),
                           xTv[:, :, t0:t0 + TC], "p1x", deps=(gev("p1x_free"),))
            sev(("p89mm", j), (ev_m, ev_x0))
            sev(("p89x0", j), ev_x0)
            sev(("p89close", j), ev_m)

        def p89_epi(j):
            t0 = j * TC
            ev_m = gev(("p89close", j))
            ev_x0 = gev(("p89x0", j))
            ev_at = []
            for m in range(NKD):
                evn = em.dma("sync", e_xn1[:, (m % 2) * TC:(m % 2) * TC + TC],
                             xn1v[:, m, t0:t0 + TC], f"exn{m % 2}",
                             deps=(gev(("xn1", j)), gev(("e_xn1", m % 2))))
                ev_at.append(vop(lambda e, mm_=m: e.scalar_tensor_tensor(
                    out=e_at[:, mm_ * TC:(mm_ + 1) * TC], in0=ps[mm_][:, 0:TC],
                    scalar=projb[:, mm_:mm_ + 1],
                    in1=e_xn1[:, (mm_ % 2) * TC:(mm_ % 2) * TC + TC],
                    op0=ALU.add, op1=ALU.add),
                    deps=(ev_m, evn, gev("e_at_free"), evc)))
                sev(("e_xn1", m % 2), ev_at[m])
                bank_used(m, ev_at[m])
            ers2, enm2 = stats(e_at, TC, deps=tuple(ev_at))
            rs3 = bcb[:, 0:TC].unsqueeze(1).broadcast_to([128, NKD, TC])
            nm3 = bcb[:, TC:2 * TC].unsqueeze(1).broadcast_to([128, NKD, TC])
            at3 = e_at.ap().rearrange("p (m t) -> p m t", m=NKD)
            x13 = e_x1.ap().rearrange("p (m t) -> p m t", m=NKD)
            x33 = e_x3.ap().rearrange("p (m t) -> p m t", m=NKD)
            e1 = vop(lambda e: e.tensor_tensor(out=x13, in0=at3, in1=rs3,
                                               op=ALU.mult),
                     deps=(ers2, gev("e_x1_free")) + tuple(ev_at))
            e2 = vop(lambda e: e.tensor_tensor(out=x13, in0=x13, in1=nm3,
                                               op=ALU.add), deps=(enm2, e1))
            e3 = vop(lambda e: e.tensor_tensor(out=e_x1[:], in0=e_x1[:],
                                               in1=p1x[:], op=ALU.add),
                     deps=(e2, ev_x0))
            ev_x1 = [e3] * NKD
            sev("e_at_free", e3)
            sev("p1x_free", e3)
            sev("bcb_free", e3)
            ers3, enm3 = stats(e_x1, TC, deps=(e3,), scl=QSCL)
            e1 = vop(lambda e: e.tensor_tensor(out=x33, in0=x13, in1=rs3,
                                               op=ALU.mult),
                     deps=(ers3, e3, gev("e_x3_free")))
            e2 = vop(lambda e: e.tensor_tensor(out=x33, in0=x33, in1=nm3,
                                               op=ALU.add), deps=(enm3, e1))
            ev_x3 = [e2] * NKD
            sev("bcb_free", e2)

            all_dn = []
            ev_hp = {}
            ev_g = None
            for m in range(NKH):
                ws = m % 2
                srcg = wgT.rearrange("(kt p) h -> p kt h", p=128)[
                    :, :, m * 128:(m + 1) * 128]
                srcu = wuT.rearrange("(kt p) h -> p kt h", p=128)[
                    :, :, m * 128:(m + 1) * 128]
                ev_wg_ = em.dma("gpsimd",
                                e_wg[:, ws * 1024:(ws + 1) * 1024]
                                .rearrange("p (kt c) -> p kt c", c=128),
                                srcg, f"ewg{ws}", deps=(gev(("e_wg", ws)),))
                ev_wu_ = em.dma("gpsimd",
                                e_wu[:, ws * 1024:(ws + 1) * 1024]
                                .rearrange("p (kt c) -> p kt c", c=128),
                                srcu, f"ewu{ws}", deps=(gev(("e_wu", ws)),))
                pgb = 0 + (m % 2)
                pub = 2 + (m % 2)
                pg = ps[pgb][:, 0:TC]
                pu = ps[pub][:, 0:TC]
                lg = [(pg, e_wg[:, ws * 1024 + kt * 128:ws * 1024 + (kt + 1) * 128],
                       e_x3[:, kt * TC:(kt + 1) * TC]) for kt in range(NKD)]
                lu = [(pu, e_wu[:, ws * 1024 + kt * 128:ws * 1024 + (kt + 1) * 128],
                       e_x3[:, kt * TC:(kt + 1) * TC]) for kt in range(NKD)]
                ev_g = mmg(lg, [pgb], deps=(ev_wg_, ev_x3[-1]))
                ev_u_ = mmg(lu, [pub], deps=(ev_wu_,))
                sev(("e_wg", ws), ev_g)
                sev(("e_wu", ws), ev_u_)
                gsl = e_gt[:, (m % 2) * TC:(m % 2) * TC + TC]
                ev_si = sop(lambda e, o=gsl, i=pg: e.activation(out=o, in_=i,
                                                               func=ACT.Silu),
                            deps=(ev_g, gev(("e_gt", m % 2))))
                bank_used(pgb, ev_si)
                ev_h = vop(lambda e, mm_=m, g=gsl, u=pu: e.scalar_tensor_tensor(
                    out=e_hp[:, mm_ * TC:(mm_ + 1) * TC], in0=g, scalar=QSCL,
                    in1=u, op0=ALU.mult, op1=ALU.mult),
                    deps=(ev_si, ev_u_, gev(("e_hp", m))))
                bank_used(pub, ev_h)
                sev(("e_gt", m % 2), ev_h)
                ev_hp[m] = ev_h
            ev_dn = {}
            for sw in range(2):
                ev = None
                for m in range(NKH):
                    ws = m % 2
                    src = wdT.rearrange("(m p) o -> m p o", p=128)[
                        m, :, sw * 512:(sw + 1) * 512]
                    ev_wd_ = em.dma("gpsimd", e_wd[:, ws * 512:(ws + 1) * 512],
                                    src, f"ewd{ws}", deps=(gev(("e_wd", ws)),))
                    for m2 in range(4):
                        g = sw * 4 + m2
                        bk = 4 + m2
                        dps = [ev_wd_, ev_hp[m]]
                        if m == 0:
                            dps.append(bank_free[bk])
                        ev = em.op("tensor",
                                   (lambda e, b=bk, mm_=m, ws_=ws, c2=m2,
                                    s0=(m == 0), s1=(m == NKH - 1):
                                    e.matmul(ps[b][:, 0:TC],
                                             e_wd[:, ws_ * 512 + c2 * 128:
                                                  ws_ * 512 + c2 * 128 + 128],
                                             e_hp[:, mm_ * TC:(mm_ + 1) * TC],
                                             start=s0, stop=s1)),
                                   inc=("tv" if (m == NKH - 1 or m2 == 3) else None),
                                   deps=tuple(dps))
                        if m == NKH - 1:
                            ev_dn[g] = ev
                    sev(("e_wd", ws), ev)
                for m2 in range(4):
                    g = sw * 4 + m2
                    ev_e = sop(lambda e, gg=g, b=4 + m2: e.copy(
                        out=e_at[:, gg * TC:(gg + 1) * TC],
                        in_=ps[b][:, 0:TC]),
                        deps=(ev_dn[g], gev("e_ffn_free")))
                    bank_used(4 + m2, ev_e)
                    ev_dn[g] = ev_e
                    all_dn.append(ev_e)
            for m in range(NKH):
                sev(("e_hp", m), ev_dn[NKD - 1])
            ers4, enm4 = stats(e_at, TC, deps=tuple(all_dn))
            e1 = vop(lambda e: e.tensor_tensor(out=x33, in0=at3, in1=rs3,
                                               op=ALU.mult),
                     deps=(ers4, gev("x3_out_free"), ev_g))
            e2 = vop(lambda e: e.tensor_tensor(out=x33, in0=x33, in1=nm3,
                                               op=ALU.add), deps=(enm4, e1))
            e3 = vop(lambda e: e.tensor_tensor(out=e_x3[:], in0=e_x3[:],
                                               in1=e_x1[:], op=ALU.add),
                     deps=(e2, ev_x1[-1]))
            ev_o = [e3] * NKD
            sev("e_ffn_free", ev_o[-1])
            sev("e_x1_free", ev_o[-1])
            sev("e_at_free", ev_o[-1])
            sev("bcb_free", ev_o[-1])
            ev_out = em.dma("gpsimd", yTv[:, :, t0:t0 + TC],
                            e_x3.ap().rearrange("p (kt t) -> p kt t", t=TC), "do",
                            deps=(ev_o[-1],))
            sev("x3_out_free", ev_out)
            sev("e_x3_free", ev_out)

        for j in range(NCH):
            in_proj_chunk("f", True, j)
            xp_dtp_chunk("f", j)
            if j > 0:
                scan_chunk("f", True, j - 1)
        scan_chunk("f", True, NCH - 1)
        for j in range(NCH - 1, -1, -1):
            if j + 2 < NCH:
                p89_epi(j + 2)
            in_proj_chunk("b", False, j)
            xp_dtp_chunk("b", j)
            if j + 1 < NCH:
                scan_chunk("b", False, j + 1)
                p89_mm(j + 1)
        scan_chunk("b", False, 0)
        p89_epi(1)
        p89_mm(0)
        p89_epi(0)

        em.op("gpsimd", lambda e: e.engine_nop(),
              deps=(("do", em.cnt.get("do", 0)),))

        sems = {}
        for name in em.cnt:
            sems[name] = stack.enter_context(nc.semaphore(name))
        blk = stack.enter_context(nc.Block())
        em.replay(blk, sems)
    lower_extended_insts(nc)
    return nc


def _col_form(v, nt):
    return np.ascontiguousarray(np.asarray(v, np.float32).reshape(nt, 128).T)


def _prep_core_inputs(xb, ins):
    m = {}
    m["xT"] = np.ascontiguousarray(np.asarray(xb, np.float32).T)
    for d, sfx in (("f", "_f"), ("b", "_b")):
        m[f"inwT_{d}"] = np.ascontiguousarray(np.asarray(ins[f"L0_attn_in_w{sfx}"], np.float32).T)
        m[f"outw_{d}"] = np.ascontiguousarray(np.asarray(ins[f"L0_attn_out_w{sfx}"], np.float32))
        cwv = np.asarray(ins[f"L0_attn_conv_w{sfx}"], np.float32).reshape(DI, 4)
        m[f"cw_{d}"] = np.ascontiguousarray(
            cwv.reshape(NCT, 128, 4).transpose(1, 0, 2).reshape(128, NCT * 4))
        m[f"cb_{d}"] = _col_form(ins[f"L0_attn_conv_b{sfx}"], NCT)
        xpv = np.asarray(ins[f"L0_attn_xp_w{sfx}"], np.float32).T
        m[f"xpw_{d}"] = np.ascontiguousarray(
            xpv.reshape(NCT, 128, 96).transpose(1, 0, 2).reshape(128, NCT * 96))
        m[f"dtpw_{d}"] = np.ascontiguousarray(np.asarray(ins[f"L0_attn_dtp_w{sfx}"], np.float32).T)
        m[f"dtpb_{d}"] = _col_form(ins[f"L0_attn_dtp_b{sfx}"], NCT)
        av = np.asarray(ins[f"L0_attn_A_log{sfx}"], np.float32)
        m[f"A_{d}"] = np.ascontiguousarray(
            av.reshape(NCT, 128, NS).transpose(1, 0, 2).reshape(128, NCT * NS))
        m[f"Dsk_{d}"] = _col_form(ins[f"L0_attn_D{sfx}"], NCT)
    m["gamR"] = _col_form(ins["L0_attn_ln_gamma"], NKD)
    m["betR"] = _col_form(ins["L0_attn_ln_beta"], NKD)
    m["projT"] = np.ascontiguousarray(np.asarray(ins["L0_attn_proj_w"], np.float32).T)
    m["projbR"] = _col_form(ins["L0_attn_proj_b"], NKD)
    m["wgT"] = np.ascontiguousarray(np.asarray(ins["L0_ffn_w_gate"], np.int32).T)
    m["wuT"] = np.ascontiguousarray(np.asarray(ins["L0_ffn_w_up"], np.int32).T)
    m["wdT"] = np.ascontiguousarray(np.asarray(ins["L0_ffn_w_down"], np.int32).T)
    return m


def kernel(**inputs):
    x = np.asarray(inputs["L0_attn_x"])
    B, L, _ = x.shape
    nc = build_nc(L=L)
    in_maps = [_prep_core_inputs(x[b], inputs) for b in range(B)]
    res = run_bass_kernel_spmd(nc, in_maps, core_ids=list(range(B)))
    out = np.stack([np.asarray(r["yT"]).T for r in res.results], axis=0)
    return np.ascontiguousarray(out.astype(np.float32))


# version 21
# speedup vs baseline: 1.0420x; 1.0229x over previous
import numpy as np
import concourse.bass as bass
import concourse.mybir as mybir
from concourse.bass_utils import run_bass_kernel_spmd
from concourse.bass_isa import ReduceOp
from concourse import library_config
from concourse.library_overlay import lower_extended_insts
from contextlib import ExitStack

F32 = mybir.dt.float32
I32 = mybir.dt.int32
ALU = mybir.AluOpType
ACT = mybir.ActivationFunctionType
AX = mybir.AxisListType

D = 1024
DI = 2048
NS = 16
DTR = 64
HID = 3072
NKD = D // 128
NCT = DI // 128
NKH = HID // 128
EPS = 1e-5
QSCL = float(np.float32(1.0 / 127.0))

ENGINES = ("sync", "tensor", "vector", "scalar", "gpsimd")


class EM:
    def __init__(self):
        self.q = {e: [] for e in ENGINES}
        self.cnt = {}
        self.mark = {}

    def op(self, eng, fn, inc=None, deps=(), amt=1):
        flat = []
        for dp in deps:
            if dp is None:
                continue
            if isinstance(dp, (list, tuple)) and dp and isinstance(dp[0], (list, tuple)):
                flat.extend(x for x in dp if x is not None)
            else:
                flat.append(dp)
        for dp in flat:
            s, v = dp
            if self.mark.get((eng, s), 0) < v:
                self.q[eng].append(("w", s, v))
                self.mark[(eng, s)] = v
        ev = None
        if inc is not None:
            self.cnt[inc] = self.cnt.get(inc, 0) + amt
            ev = (inc, self.cnt[inc])
        self.q[eng].append(("o", fn, inc, amt))
        return ev

    def dma(self, eng, out, in_, sem, deps=()):
        prior = self.cnt.get(sem, 0)
        return self.op(eng, lambda e, o=out, i=in_: e.dma_start(out=o, in_=i),
                       inc=sem, deps=deps + ((sem, prior),), amt=16)

    def replay(self, block, sems):
        for name in ENGINES:
            section = getattr(block, name)

            def make(qname):
                def body(eng):
                    for item in self.q[qname]:
                        if item[0] == "w":
                            eng.wait_ge(sems[item[1]], item[2])
                        else:
                            inst = item[1](eng)
                            if item[2] is not None:
                                inst.then_inc(sems[item[2]], item[3])
                return body

            section(make(name))


def build_nc(L=4096, taps=False):
    TC = 256
    NCH = L // TC
    TS = TC
    TC9 = 128

    nc = bass.Bass("TRN2", target_bir_lowering=False, debug=False)

    dram = lambda n, s, dt=F32, k="ExternalInput": nc.dram_tensor(n, s, dt, kind=k).ap()
    xT = dram("xT", [D, L])
    inwT = {d: dram(f"inwT_{d}", [D, 2 * DI]) for d in "fb"}
    outw = {d: dram(f"outw_{d}", [D, DI]) for d in "fb"}
    cwR = {d: dram(f"cw_{d}", [128, NCT * 4]) for d in "fb"}
    cbR = {d: dram(f"cb_{d}", [128, NCT]) for d in "fb"}
    xpwR = {d: dram(f"xpw_{d}", [128, NCT * 96]) for d in "fb"}
    dtpwR = {d: dram(f"dtpw_{d}", [DTR, DI]) for d in "fb"}
    dtpbR = {d: dram(f"dtpb_{d}", [128, NCT]) for d in "fb"}
    AR = {d: dram(f"A_{d}", [128, NCT * NS]) for d in "fb"}
    DskR = {d: dram(f"Dsk_{d}", [128, NCT]) for d in "fb"}
    gamR = dram("gamR", [128, NKD])
    betR = dram("betR", [128, NKD])
    projTD = dram("projT", [2 * D, D])
    projbR = dram("projbR", [128, NKD])
    wgT = dram("wgT", [D, HID], I32)
    wuT = dram("wuT", [D, HID], I32)
    wdT = dram("wdT", [HID, D], I32)

    ks = "ExternalOutput" if taps else "Internal"
    WeT = {d: dram(f"WeT_{d}", [DI, D], F32, ks) for d in "fb"}
    xn1S = dram("xn1S", [D, L], F32, ks)
    xn2S = dram("xn2S", [D, L], F32, ks)
    uS = {d: dram(f"uS_{d}", [DI, L], F32, ks) for d in "fb"}
    zaS = {d: dram(f"zaS_{d}", [DI, L], F32, ks) for d in "fb"}
    dlS = {d: dram(f"dlS_{d}", [DI, L], F32, ks) for d in "fb"}
    bcSD = {d: dram(f"bcS_{d}", [32, L], F32, ks) for d in "fb"}
    ypS = {d: dram(f"ypS_{d}", [DI, L], F32, ks) for d in "fb"}
    yT = dram("yT", [D, L], F32, "ExternalOutput")

    def dview(t, p=128):
        return t.rearrange("(a p) t -> p a t", p=p)
    xTv = dview(xT)
    xn1v = dview(xn1S)
    xn2v = dview(xn2S)
    yTv = dview(yT)
    uV = {d: dview(uS[d]) for d in "fb"}
    zaV = {d: dview(zaS[d]) for d in "fb"}
    dlV = {d: dview(dlS[d]) for d in "fb"}
    ypV = {d: dview(ypS[d]) for d in "fb"}
    WeV = {d: dview(WeT[d]) for d in "fb"}

    em = EM()
    st = {}
    sev = st.__setitem__
    gev = st.get

    stack = ExitStack()
    sb = lambda name, shape, dt=F32: stack.enter_context(nc.sbuf_tensor(name, shape, dt))
    psmk = lambda name, shape: stack.enter_context(nc.psum_tensor(name, shape, F32))

    with stack:
        cw = {d: sb(f"cw{d}", [128, NCT * 4]) for d in "fb"}
        cb = {d: sb(f"cb{d}", [128, NCT]) for d in "fb"}
        dtpb = {d: sb(f"dtpb{d}", [128, NCT]) for d in "fb"}
        As = {d: sb(f"As{d}", [128, NCT * NS]) for d in "fb"}
        Dsk = {d: sb(f"Dsk{d}", [128, NCT]) for d in "fb"}
        projb = sb("projb", [128, NKD])
        gam = sb("gam", [128, NKD])
        bet = sb("bet", [128, NKD])
        hst = {d: sb(f"hst{d}", [128, NCT * NS]) for d in "fb"}
        halo = {d: sb(f"halo{d}", [128, NCT * 4]) for d in "fb"}

        p1x = sb("p1x", [128, NKD * TC])
        sqb = sb("sqb", [128, 2 * TC])
        bcb = sb("bcb", [128, 2 * TC])

        ipslab = sb("ipslab", [128, 2 * 1024])
        ipxn2 = sb("ipxn2", [128, NKD * TC])
        ipring = sb("ipring", [128, 2 * (TC + 8)])
        ipcvt = sb("ipcvt", [128, TC])
        ipout = sb("ipout", [128, 2 * TC])

        p4u = sb("p4u", [128, 2 * TC])
        p4dt = sb("p4dt", [64, TC])
        p4z0 = sb("p4z0", [128, 2 * TC])
        p4sp = sb("p4sp", [128, TC])
        p4xpw = sb("p4xpw", [128, 2 * 96])
        p4bc = sb("p4bc", [128, TC])
        p4dw = sb("p4dw", [64, 2 * 128])

        scdl = sb("scdl", [128, 4 * TS])
        scu = sb("scu", [128, 4 * TS])
        scza = sb("scza", [128, 4 * TS])
        scdA = sb("scdA", [128, (NS // 2) * TS])
        scbh = sb("scbh", [128, NS * TS])
        scB = sb("scB", [128, NS * TS])
        scC = sb("scC", [128, NS * TS])
        scdu = sb("scdu", [128, TS])
        scy = sb("scy", [128, TS])
        scyp = sb("scyp", [128, 2 * TS])

        e_yf = sb("e_yf", [128, 4 * TC])
        e_yb = sb("e_yb", [128, 4 * TC])
        e_we = sb("e_we", [128, 2 * 1024])
        e_xn1 = sb("e_xn1", [128, 2 * TC])
        e_at = sb("e_at", [128, NKD * TC])
        e_x1 = sb("e_x1", [128, NKD * TC])
        e_x3 = sb("e_x3", [128, NKD * TC])
        e_hp = sb("e_hp", [128, NKH * TC])
        e_gt = sb("e_gt", [128, 2 * TC])
        e_wg = sb("e_wg", [128, 2 * 1024])
        e_wu = sb("e_wu", [128, 2 * 1024])
        e_wd = sb("e_wd", [128, 2 * 512])

        ps = [psmk(f"ps{i}", [128, 512]) for i in range(8)]

        blk_holder = []

        def prj_slice(kt, n2):
            buf = (ipxn2, p1x, e_at, e_x1)[kt // 2]
            off = (kt % 2) * 1024 + n2 * 512
            return buf[:, off:off + 512]

        def vop(fn, deps=(), inc="vv"):
            return em.op("vector", fn, inc=inc, deps=deps)

        def sop(fn, deps=(), inc="sv"):
            return em.op("scalar", fn, inc=inc, deps=deps)

        bank_free = {i: None for i in range(8)}

        def mmg(lst, banks, deps=(), inc="tv"):
            bdeps = tuple(bank_free[b] for b in banks)
            ev = None
            n = len(lst)
            for i, (o, l, r) in enumerate(lst):
                ev = em.op("tensor",
                           (lambda e, oo=o, ll=l, rr=r, s0=(i == 0), s1=(i == n - 1):
                            e.matmul(oo, ll, rr, start=s0, stop=s1)),
                           inc=(inc if i == n - 1 else None),
                           deps=(deps + bdeps) if i == 0 else ())
            return ev

        def bank_used(b, ev):
            bank_free[b] = ev

        def gop(fn, deps=(), inc="gv"):
            return em.op("gpsimd", fn, inc=inc, deps=deps)

        def stats(xbuf, T, deps, scl=1.0):
            assert T == TC
            dep_bcb = (gev("bcb_free"),)
            rsg = bcb[:, 0:T]
            xsm = bcb[:, TC:TC + T]
            x3 = xbuf[:].rearrange("p (a t) -> p t a", a=NKD)
            e_xs = vop(lambda e: e.tensor_reduce(out=xsm, in_=x3, axis=AX.X,
                                                 op=ALU.add), deps=deps + dep_bcb)
            e_arx = gop(lambda e: e.partition_all_reduce(xsm, xsm, 128,
                                                         ReduceOp.add),
                        deps=(e_xs,))
            e_sq0 = sop(lambda e: e.activation(out=rsg, in_=xbuf[:, 0:T],
                                               func=ACT.Square),
                        deps=deps + dep_bcb)
            e_add = e_sq0
            for kt in range(1, NKD):
                sl = sqb[:, (kt % 2) * TC:(kt % 2) * TC + T]
                e_sq = sop(lambda e, k=kt, o=sl: e.activation(
                    out=o, in_=xbuf[:, k * TC:k * TC + T], func=ACT.Square),
                    deps=deps + (gev(("sqf", kt % 2)),))
                e_add = vop(lambda e, i=sl: e.tensor_tensor(
                    out=rsg, in0=rsg, in1=i, op=ALU.add), deps=(e_sq, e_add))
                sev(("sqf", kt % 2), e_add)
            e_arq = gop(lambda e: e.partition_all_reduce(rsg, rsg, 128,
                                                         ReduceOp.add),
                        deps=(e_add,))
            t0 = sqb[:, 0:TC]
            t1 = sqb[:, TC:2 * TC]
            e_t = vop(lambda e: e.scalar_tensor_tensor(
                out=t0, in0=xsm, scalar=1.0 / D, in1=xsm,
                op0=ALU.mult, op1=ALU.mult),
                deps=(e_arx, gev(("sqf", 0))))
            e_v = vop(lambda e: e.tensor_tensor(out=t0, in0=rsg, in1=t0,
                                                op=ALU.subtract),
                      deps=(e_arq, e_t))
            e_v = vop(lambda e: e.tensor_scalar(out=t0, in0=t0, scalar1=1.0 / D,
                                                scalar2=EPS, op0=ALU.mult,
                                                op1=ALU.add), deps=(e_v,))
            e_l = sop(lambda e: e.activation(out=t1, in_=t0, func=ACT.Ln),
                      deps=(e_v, gev(("sqf", 1))))
            e_r0 = sop(lambda e: e.activation(out=rsg, in_=t1, func=ACT.Exp,
                                              scale=-0.5), deps=(e_l,))
            e_n = vop(lambda e: e.tensor_tensor(out=t1, in0=rsg, in1=rsg,
                                                op=ALU.mult), deps=(e_r0,))
            e_n2 = vop(lambda e: e.tensor_tensor(out=t1, in0=t1, in1=t0,
                                                 op=ALU.mult), deps=(e_n,))
            e_n3 = vop(lambda e: e.tensor_scalar(out=t1, in0=t1,
                                                 scalar1=-0.5 * scl,
                                                 scalar2=1.5 * scl,
                                                 op0=ALU.mult, op1=ALU.add),
                       deps=(e_n2,))
            ec1 = vop(lambda e: e.tensor_tensor(out=rsg, in0=rsg, in1=t1,
                                                op=ALU.mult), deps=(e_n3,))
            ec2 = vop(lambda e: e.scalar_tensor_tensor(
                out=xsm, in0=xsm, scalar=-1.0 / D, in1=rsg,
                op0=ALU.mult, op1=ALU.mult), deps=(ec1,))
            sev(("sqf", 0), e_n2)
            sev(("sqf", 1), ec1)
            return ec1, ec2

        em.op("gpsimd", lambda e: e.load_library(library_config.attn))
        ev_c = []
        for d in "fb":
            ev_c.append(em.dma("sync", cw[d][:], cwR[d], "cst"))
            ev_c.append(em.dma("sync", cb[d][:], cbR[d], "cst"))
            ev_c.append(em.dma("sync", dtpb[d][:], dtpbR[d], "cst"))
            ev_c.append(em.dma("sync", As[d][:], AR[d], "cst"))
            ev_c.append(em.dma("sync", Dsk[d][:], DskR[d], "cst"))
        ev_c.append(em.dma("sync", projb[:], projbR, "cst"))
        ev_c.append(em.dma("sync", gam[:], gamR, "cst"))
        ev_c.append(em.dma("sync", bet[:], betR, "cst"))
        evc = ev_c[-1]
        vop(lambda e: e.memset(hst["f"][:], 0.0))
        ev_hz = vop(lambda e: e.memset(hst["b"][:], 0.0))
        vop(lambda e: e.memset(halo["f"][:], 0.0))
        ev_halo = vop(lambda e: e.memset(halo["b"][:], 0.0))

        ev_As = {}
        for d in "fb":
            ea = sop(lambda e, dd=d: e.activation(out=As[dd][:], in_=As[dd][:],
                                                  func=ACT.Exp), deps=(evc,))
            ev_As[d] = vop(lambda e, dd=d: e.tensor_scalar_mul(
                out=As[dd][:], in0=As[dd][:], scalar1=-1.0), deps=(ea,))

        we_done = {}
        prjv = projTD.rearrange("(h kt p) o -> h p kt o", h=2, p=128)
        for di, d in enumerate("fb"):
            ev_p = []
            for kt in range(NKD):
                for n2 in range(2):
                    ev_p.append(em.dma("sync", prj_slice(kt, n2),
                                       prjv[di, :, kt, n2 * 512:(n2 + 1) * 512], "prj",
                                       deps=(gev("prj_readers"),)))
            ev_pl = ev_p[-1]
            for m in range(NCT):
                slot = m % 2
                src = outw[d].rearrange("(kt p) c -> p kt c", p=128)[
                    :, :, m * 128:(m + 1) * 128]
                ev_l = em.dma("sync",
                              ipslab[:, slot * 1024:(slot + 1) * 1024]
                              .rearrange("p (kt c) -> p kt c", c=128),
                              src, f"ipsl{slot}", deps=(gev(("p0slab", slot)),))
                ev_mm = None
                for n2 in range(2):
                    lst = [(ps[n2][:, 0:512],
                            ipslab[:, slot * 1024 + kt * 128:slot * 1024 + (kt + 1) * 128],
                            prj_slice(kt, n2)) for kt in range(NKD)]
                    ev_mm = mmg(lst, [n2], deps=(ev_l, ev_pl))
                    stg = e_hp[:, n2 * 512:(n2 + 1) * 512]
                    ev_cp = sop(lambda e, o=stg, i=ps[n2][:, 0:512]: e.copy(out=o, in_=i),
                                deps=(ev_mm, gev(("p0stg", n2))))
                    bank_used(n2, ev_cp)
                    dst = WeV[d][:, m, n2 * 512:(n2 + 1) * 512]
                    ev_wr = em.dma("scalar", dst, stg, f"p0st{n2}", deps=(ev_cp,))
                    sev(("p0stg", n2), ev_wr)
                    sev(("p0last", n2), ev_wr)
                    we_done[d] = ev_wr
                sev(("p0slab", slot), ev_mm)
            sev("prj_readers", ev_mm)
        for m_ in range(4):
            sev(("e_hp", m_), (gev(("p0last", 0)), gev(("p0last", 1))))
        for d in "fb":
            we_done[d] = (we_done[d], gev(("p0last", 0)), gev(("p0last", 1)))
        sev("prj_free", we_done["b"])

        sev("p1x_free", we_done["b"])
        sev("ipxn2_free", we_done["b"])
        sev("e_at_free", we_done["b"])
        sev("e_x1_free", we_done["b"])

        p1_order = []
        lo, hi = 0, NCH - 1
        while lo <= hi:
            p1_order.append(lo)
            lo += 1
            if lo <= hi:
                p1_order.append(hi)
                hi -= 1

        for j in p1_order:
            t0 = j * TC
            ev_x = em.dma("sync", p1x.ap().rearrange("p (kt t) -> p kt t", t=TC),
                          xTv[:, :, t0:t0 + TC], "p1x", deps=(gev("p1x_free"),))
            ers, enm = stats(p1x, TC, deps=(ev_x,))
            wlast = {}
            for kt in range(NKD):
                sx = p1x[:, kt * TC:(kt + 1) * TC]
                e1 = vop(lambda e, a=sx: e.tensor_tensor(out=a, in0=a, in1=bcb[:, 0:TC],
                                                         op=ALU.mult), deps=(ers,))
                e2 = vop(lambda e, a=sx: e.tensor_tensor(out=a, in0=a,
                                                         in1=bcb[:, TC:2 * TC],
                                                         op=ALU.add), deps=(enm, e1))
                ew1 = em.dma("gpsimd", xn1v[:, kt, t0:t0 + TC], sx, f"p1w{kt % 4}", deps=(e2,))
                e3 = vop(lambda e, a=sx, k=kt: e.scalar_tensor_tensor(
                    out=a, in0=a, scalar=gam[:, k:k + 1],
                    in1=bet[:, k:k + 1].broadcast_to([128, TC]),
                    op0=ALU.mult, op1=ALU.add), deps=(ew1, evc))
                ew2 = em.dma("gpsimd", xn2v[:, kt, t0:t0 + TC], sx, f"p1w{kt % 4}", deps=(e3,))
                wlast[kt % 4] = ew2
            comp = tuple(wlast.values())
            sev("p1x_free", comp)
            sev(("xn1", j), comp)
            sev(("xn2", j), comp)
            sev("bcb_free", e3)

        def in_proj_chunk(d, asc, j):
            if True:
                t0 = j * TC
                ev_xn = em.dma("sync", ipxn2.ap().rearrange("p (kt t) -> p kt t", t=TC),
                               xn2v[:, :, t0:t0 + TC], "ipxn2",
                               deps=(gev(("xn2", j)), gev("ipxn2_free")))
                ev_last = None
                for m in range(2 * NCT):
                    slot = m % 2
                    src = inwT[d].rearrange("(kt p) o -> p kt o", p=128)[
                        :, :, m * 128:(m + 1) * 128]
                    ev_l = em.dma("sync",
                                  ipslab[:, slot * 1024:(slot + 1) * 1024]
                                  .rearrange("p (kt c) -> p kt c", c=128),
                                  src, f"ipsl{slot}", deps=(gev(("ipslab", slot)),))
                    pm = ps[slot][:, 0:TC]
                    lst = [(pm, ipslab[:, slot * 1024 + kt * 128:slot * 1024 + (kt + 1) * 128],
                            ipxn2[:, kt * TC:(kt + 1) * TC]) for kt in range(NKD)]
                    ev_mm = mmg(lst, [slot], deps=(ev_l, ev_xn))
                    sev(("ipslab", slot), ev_mm)
                    ev_last = ev_mm
                    if m < NCT:
                        ct = m
                        ring = ipring[:, slot * (TC + 8):slot * (TC + 8) + TC + 8]
                        ev_ev = sop(lambda e, o=ring[:, 4:4 + TC], i=pm: e.copy(out=o, in_=i),
                                    deps=(ev_mm, gev(("ring_busy", slot))))
                        bank_used(slot, ev_ev)
                        cvt = ipcvt[:, 0:TC]
                        if asc:
                            ev_h = vop(lambda e, c=ct, rg=ring: e.tensor_copy(
                                out=rg[:, 1:4], in_=halo[d][:, c * 4:c * 4 + 3]),
                                deps=(gev(("halo", d, ct)), ev_halo,
                                      gev(("ring_busy", slot))))
                            e1 = vop(lambda e, c=ct, o=cvt, rg=ring: e.tensor_scalar(
                                out=o, in0=rg[:, 1:1 + TC],
                                scalar1=cw[d][:, c * 4:c * 4 + 1], scalar2=None,
                                op0=ALU.mult),
                                deps=(ev_ev, ev_h, gev("cvt_busy")))
                            for k in range(1, 4):
                                e1 = vop(lambda e, kk=k, c=ct, o=cvt, rg=ring:
                                         e.scalar_tensor_tensor(
                                             out=o, in0=rg[:, 1 + kk:1 + kk + TC],
                                             scalar=cw[d][:, c * 4 + kk:c * 4 + kk + 1],
                                             in1=o, op0=ALU.mult, op1=ALU.add), deps=(e1,))
                            ev_hs = vop(lambda e, c=ct, rg=ring: e.tensor_copy(
                                out=halo[d][:, c * 4:c * 4 + 3],
                                in_=rg[:, 1 + TC:4 + TC]), deps=(ev_ev,))
                        else:
                            ev_h = vop(lambda e, c=ct, rg=ring: e.tensor_copy(
                                out=rg[:, 4 + TC:4 + TC + 3],
                                in_=halo[d][:, c * 4:c * 4 + 3]),
                                deps=(gev(("halo", d, ct)), ev_halo,
                                      gev(("ring_busy", slot))))
                            e1 = vop(lambda e, c=ct, o=cvt, rg=ring: e.tensor_scalar(
                                out=o, in0=rg[:, 4:4 + TC],
                                scalar1=cw[d][:, c * 4 + 3:c * 4 + 4], scalar2=None,
                                op0=ALU.mult),
                                deps=(ev_ev, ev_h, gev("cvt_busy")))
                            for k in range(1, 4):
                                e1 = vop(lambda e, kk=k, c=ct, o=cvt, rg=ring:
                                         e.scalar_tensor_tensor(
                                             out=o, in0=rg[:, 4 + kk:4 + kk + TC],
                                             scalar=cw[d][:, c * 4 + 3 - kk:c * 4 + 4 - kk],
                                             in1=o, op0=ALU.mult, op1=ALU.add), deps=(e1,))
                            ev_hs = vop(lambda e, c=ct, rg=ring: e.tensor_copy(
                                out=halo[d][:, c * 4:c * 4 + 3], in_=rg[:, 4:7]),
                                deps=(ev_ev,))
                        sev(("halo", d, ct), ev_hs)
                        sev(("ring_busy", slot), e1)
                        uo = ipout[:, slot * TC:slot * TC + TC]
                        ev_u = sop(lambda e, c=ct, o=uo, i=cvt: e.activation(
                            out=o, in_=i, func=ACT.Silu, bias=cb[d][:, c:c + 1]),
                            deps=(e1, gev(("out_busy", slot))))
                        sev("cvt_busy", ev_u)
                        ev_w = em.dma("gpsimd", uV[d][:, ct, t0:t0 + TC], uo, f"ipo{slot}",
                                      deps=(ev_u,))
                        sev(("out_busy", slot), ev_w)
                        sev(("u", d, j, ct), ev_w)
                    else:
                        ct = m - NCT
                        zo = ipout[:, slot * TC:slot * TC + TC]
                        ev_z = sop(lambda e, o=zo, i=pm: e.activation(
                            out=o, in_=i, func=ACT.Silu),
                            deps=(ev_mm, gev(("out_busy", slot))))
                        bank_used(slot, ev_z)
                        ev_w = em.dma("gpsimd", zaV[d][:, ct, t0:t0 + TC], zo, f"ipo{slot}",
                                      deps=(ev_z,))
                        sev(("out_busy", slot), ev_w)
                        sev(("za", d, j, ct), ev_w)
                sev("ipxn2_free", ev_last)

        def xp_dtp_chunk(d, j):
            if True:
                t0 = j * TC
                ev_mm = None
                for ct in range(NCT):
                    slot = ct % 2
                    ev_l = em.dma("sync", p4u[:, slot * TC:slot * TC + TC],
                                  uV[d][:, ct, t0:t0 + TC], f"p4u{slot}",
                                  deps=(gev(("u", d, j, ct)), gev(("p4u", slot))))
                    ev_xw = em.dma("sync", p4xpw[:, slot * 96:slot * 96 + 96],
                                   xpwR[d][:, ct * 96:(ct + 1) * 96], f"p4xw{slot}",
                                   deps=(gev(("p4xpw", slot)),))
                    ev_mm = em.op("tensor",
                                  (lambda e, sl=slot, s0=(ct == 0), s1=(ct == NCT - 1):
                                   e.matmul(ps[2][0:96, 0:TC],
                                            p4xpw[:, sl * 96:sl * 96 + 96],
                                            p4u[:, sl * TC:sl * TC + TC],
                                            start=s0, stop=s1)),
                                  inc="tv",
                                  deps=((ev_l, ev_xw, bank_free[2]) if ct == 0
                                        else (ev_l, ev_xw)))
                    sev(("p4u", slot), ev_mm)
                    sev(("p4xpw", slot), ev_mm)
                ev_dt = sop(lambda e: e.copy(out=p4dt[:, 0:TC], in_=ps[2][0:DTR, 0:TC]),
                            deps=(ev_mm, gev("p4dt_free")))
                ev_bcc = sop(lambda e: e.copy(out=p4bc[DTR:96, 0:TC],
                                              in_=ps[2][DTR:96, 0:TC]),
                             deps=(ev_mm, gev("p4bc_free")))
                ev_bc = em.dma("scalar", bcSD[d][:, t0:t0 + TC], p4bc[DTR:96, 0:TC], "p4bc",
                               deps=(ev_bcc,))
                sev(("bc", d, j), ev_bc)
                sev("p4bc_free", ev_bc)
                bank_used(2, max(ev_bcc, ev_dt, key=lambda x: 0))
                ev_d = None
                for ct in range(NCT):
                    slot = ct % 2
                    ev_dw_ = em.dma("sync", p4dw[0:DTR, slot * 128:slot * 128 + 128],
                                    dtpwR[d][:, ct * 128:(ct + 1) * 128], f"p4dw{slot}",
                                    deps=(gev(("p4dw", slot)),))
                    pm = ps[3 + slot][:, 0:TC]
                    ev_d = mmg([(pm, p4dw[0:DTR, slot * 128:slot * 128 + 128],
                                 p4dt[:, 0:TC])], [3 + slot],
                               deps=(ev_dt, ev_dw_))
                    sev(("p4dw", slot), ev_d)
                    z0 = p4z0[:, slot * TC:slot * TC + TC]
                    e1 = vop(lambda e, o=z0, i=pm, c=ct: e.tensor_scalar_add(
                        out=o, in0=i, scalar1=dtpb[d][:, c:c + 1]),
                        deps=(ev_d, gev(("p4z0", slot))))
                    bank_used(3 + slot, e1)
                    e2 = sop(lambda e, i=z0: e.activation(
                        out=p4sp[:, 0:TC], in_=i, func=ACT.Abs), deps=(e1,))
                    e3 = sop(lambda e: e.activation(
                        out=p4sp[:, 0:TC], in_=p4sp[:, 0:TC], func=ACT.Exp, scale=-1.0),
                        deps=(e2,))
                    e4 = sop(lambda e: e.activation(
                        out=p4sp[:, 0:TC], in_=p4sp[:, 0:TC], func=ACT.Ln, bias=1.0),
                        deps=(e3,))
                    e5 = vop(lambda e, o=z0: e.scalar_tensor_tensor(
                        out=o, in0=o, scalar=0.0, in1=p4sp[:, 0:TC],
                        op0=ALU.max, op1=ALU.add), deps=(e4,))
                    ev_w = em.dma("gpsimd", dlV[d][:, ct, t0:t0 + TC], z0, f"p4z{slot}",
                                  deps=(e5,))
                    sev(("p4z0", slot), ev_w)
                    sev(("dl", d, j, ct), ev_w)
                sev("p4dt_free", ev_d)

        def scan_chunk(d, asc, j):
            rev = not asc
            if True:
                a0 = j * TC
                ev_B = [None] * NS
                ev_C = [None] * NS
                for n in range(NS):
                    bs = n % 4
                    srcB = bcSD[d][n:n + 1, a0:a0 + TS].partition_broadcast(128).squeeze(1)
                    ev_B[n] = em.dma("scalar", scB[:, n * TS:(n + 1) * TS], srcB, f"scB{bs}",
                                     deps=(gev(("bc", d, j)), gev(("scB", n))))
                    srcC = bcSD[d][NS + n:NS + n + 1, a0:a0 + TS]\
                        .partition_broadcast(128).squeeze(1)
                    ev_C[n] = em.dma("scalar", scC[:, n * TS:(n + 1) * TS], srcC, f"scC{bs}",
                                     deps=(gev(("bc", d, j)), gev(("scC", n))))
                pair_ev = [None, None, None]
                for ct in range(NCT):
                    slot = ct % 4
                    if ct % 2 == 0:
                        ps_ = (ct // 2) % 2
                        pair_ev[0] = em.dma(
                            "sync",
                            scdl[:, ps_ * 2 * TS:(ps_ + 1) * 2 * TS]
                            .rearrange("p (c t) -> p c t", c=2),
                            dlV[d][:, ct:ct + 2, a0:a0 + TS], f"scdl{ps_}",
                            deps=(gev(("dl", d, j, ct)), gev(("dl", d, j, ct + 1)),
                                  gev(("scdl", ps_))))
                        pair_ev[1] = em.dma(
                            "sync",
                            scu[:, ps_ * 2 * TS:(ps_ + 1) * 2 * TS]
                            .rearrange("p (c t) -> p c t", c=2),
                            uV[d][:, ct:ct + 2, a0:a0 + TS], f"scu{ps_}",
                            deps=(gev(("u", d, j, ct)), gev(("u", d, j, ct + 1)),
                                  gev(("scu", ps_))))
                        pair_ev[2] = em.dma(
                            "sync",
                            scza[:, ps_ * 2 * TS:(ps_ + 1) * 2 * TS]
                            .rearrange("p (c t) -> p c t", c=2),
                            zaV[d][:, ct:ct + 2, a0:a0 + TS], f"scza{ps_}",
                            deps=(gev(("za", d, j, ct)), gev(("za", d, j, ct + 1)),
                                  gev(("scza", ps_))))
                    ev_dl, ev_su, ev_sz = pair_ev
                    dl = scdl[:, slot * TS:slot * TS + TS]
                    uu = scu[:, slot * TS:slot * TS + TS]
                    za = scza[:, slot * TS:slot * TS + TS]
                    ev_da = [None] * NS

                    def emit_da(n, extra=()):
                        sl8 = n % 8
                        ev_da[n] = sop(lambda e, nn=n, s8=sl8, c=ct, i=dl:
                                       e.activation(
                                           out=scdA[:, s8 * TS:(s8 + 1) * TS],
                                           in_=i, func=ACT.Exp,
                                           scale=As[d][:, c * NS + nn:
                                                       c * NS + nn + 1]),
                                       deps=((ev_dl, ev_As[d]) if n == 0 else ())
                                            + (gev(("scdA", sl8)),) + extra)

                    for n in range(8):
                        emit_da(n)
                    ev_du = vop(lambda e, i=dl, u=uu: e.tensor_tensor(
                        out=scdu[:, 0:TS], in0=i, in1=u, op=ALU.mult),
                        deps=(ev_dl, ev_su, gev("scdu_free")))
                    bh3 = scbh.ap().rearrange("p (n t) -> p n t", n=NS)
                    du3 = scdu[:, 0:TS].unsqueeze(1).broadcast_to([128, NS, TS])
                    sB3 = scB.ap().rearrange("p (n t) -> p n t", n=NS)
                    ev_b = vop(lambda e: e.tensor_tensor(
                        out=bh3, in0=du3, in1=sB3, op=ALU.mult),
                        deps=(ev_du, gev("scbh_free")) + tuple(ev_B))
                    if ct == NCT - 1:
                        for n in range(NS):
                            sev(("scB", n), ev_b)
                    ev_scan = [None] * NS
                    for n in range(NS):
                        a_sl = scdA[:, (n % 8) * TS:(n % 8 + 1) * TS]
                        b_sl = scbh[:, n * TS:(n + 1) * TS]
                        if rev:
                            a_sl = a_sl[:, ::-1]
                            b_sl = b_sl[:, ::-1]
                        ev_scan[n] = vop(lambda e, a=a_sl, b=b_sl, c=ct, nn=n:
                                         e.tensor_tensor_scan(
                                             out=b, data0=a, data1=b,
                                             initial=hst[d][:, c * NS + nn:c * NS + nn + 1],
                                             op0=ALU.mult, op1=ALU.add),
                                         deps=(ev_b, ev_da[n], ev_hz))
                        sev(("scdA", n % 8), ev_scan[n])
                        if n < 8:
                            emit_da(n + 8, extra=(ev_da[n],))
                    bcol = 0 if rev else TS - 1
                    ev_hst = vop(lambda e, c=ct: e.tensor_copy(
                        out=hst[d][:, c * NS:(c + 1) * NS],
                        in_=scbh.ap().rearrange("p (n t) -> p n t", n=NS)[:, :, bcol]),
                        deps=(ev_scan[-1],))
                    sC3 = scC.ap().rearrange("p (n t) -> p n t", n=NS)
                    ev_hcb = vop(lambda e: e.tensor_tensor(
                        out=bh3, in0=bh3, in1=sC3, op=ALU.mult),
                        deps=(ev_scan[-1], ev_hst) + tuple(ev_C))
                    ev_hc = [ev_hcb] * NS
                    if ct == NCT - 1:
                        for n in range(NS):
                            sev(("scC", n), ev_hcb)
                    hc_v = scbh.ap().rearrange("p (n t) -> p t n", n=NS)
                    ev_y = vop(lambda e: e.tensor_reduce(out=scy[:, 0:TS], in_=hc_v,
                                                         axis=AX.X, op=ALU.add),
                               deps=(ev_hc[-1],))
                    sev("scbh_free", ev_y)
                    ev_yp = vop(lambda e, c=ct, u=uu: e.scalar_tensor_tensor(
                        out=scy[:, 0:TS], in0=u, scalar=Dsk[d][:, c:c + 1],
                        in1=scy[:, 0:TS], op0=ALU.mult, op1=ALU.add),
                        deps=(ev_y, ev_su, evc))
                    sev("scdu_free", ev_yp)
                    if ct % 2 == 1:
                        sev(("scdl", (ct // 2) % 2), ev_yp)
                        sev(("scu", (ct // 2) % 2), ev_yp)
                    ys = scyp[:, (ct % 2) * TS:(ct % 2) * TS + TS]
                    ev_ypp = vop(lambda e, o=ys, z=za: e.tensor_tensor(
                        out=o, in0=scy[:, 0:TS], in1=z, op=ALU.mult),
                        deps=(ev_yp, ev_sz, gev("scyp")))
                    if ct % 2 == 1:
                        sev(("scza", (ct // 2) % 2), ev_ypp)
                    if ct % 2 == 1:
                        ev_w = em.dma("gpsimd",
                                      ypV[d][:, ct - 1:ct + 1, a0:a0 + TS],
                                      scyp.ap().rearrange("p (c t) -> p c t", c=2),
                                      "scypw", deps=(ev_ypp,))
                        sev("scyp", ev_w)
                        sev(("yp", d, j, ct - 1), ev_w)
                        sev(("yp", d, j, ct), ev_w)

        def p89_mm(j):
            t0 = j * TC
            ev_m = None
            for di, d in enumerate("fb"):
                ybuf = e_yf if d == "f" else e_yb
                for ct0 in range(0, NCT, 2):
                    pslot = (ct0 // 2) % 2
                    ev_yl = em.dma("sync",
                                   ybuf[:, pslot * 2 * TC:(pslot + 1) * 2 * TC]
                                   .rearrange("p (c t) -> p c t", c=2),
                                   ypV[d][:, ct0:ct0 + 2, t0:t0 + TC], f"ey{d}{pslot}",
                                   deps=(gev(("yp", d, j, ct0)),
                                         gev(("yp", d, j, ct0 + 1)),
                                         gev((d + "yb", pslot))))
                    for sub in range(2):
                        ct = ct0 + sub
                        slot = pslot * 2 + sub
                        wslot = (di * NCT + ct) % 2
                        ev_wl = em.dma("sync", e_we[:, wslot * 1024:(wslot + 1) * 1024],
                                       WeV[d][:, ct, :], f"ewe{wslot}",
                                       deps=(we_done[d], gev(("e_we", wslot))))
                        for m in range(NKD):
                            s0 = (di == 0 and ct == 0)
                            s1 = (di == 1 and ct == NCT - 1)
                            dps = [ev_yl, ev_wl]
                            if s0:
                                dps += [bank_free[b] for b in range(NKD)]
                            ev_m = em.op("tensor",
                                         (lambda e, mm_=m, sl=slot, ws=wslot, st0=s0,
                                          st1=s1, yb=ybuf:
                                          e.matmul(ps[mm_][:, 0:TC],
                                                   e_we[:, ws * 1024 + mm_ * 128:
                                                        ws * 1024 + (mm_ + 1) * 128],
                                                   yb[:, sl * TC:sl * TC + TC],
                                                   start=st0, stop=st1)),
                                         inc=("tv" if (s1 or m == NKD - 1) else None),
                                         deps=tuple(dps))
                        sev(("e_we", wslot), ev_m)
                    sev((d + "yb", pslot), ev_m)
            ev_x0 = em.dma("sync", p1x.ap().rearrange("p (kt t) -> p kt t", t=TC),
                           xTv[:, :, t0:t0 + TC], "p1x", deps=(gev("p1x_free"),))
            sev(("p89mm", j), (ev_m, ev_x0))
            sev(("p89x0", j), ev_x0)
            sev(("p89close", j), ev_m)

        def p89_epi(j):
            t0 = j * TC
            ev_m = gev(("p89close", j))
            ev_x0 = gev(("p89x0", j))
            ev_at = []
            for m in range(NKD):
                evn = em.dma("sync", e_xn1[:, (m % 2) * TC:(m % 2) * TC + TC],
                             xn1v[:, m, t0:t0 + TC], f"exn{m % 2}",
                             deps=(gev(("xn1", j)), gev(("e_xn1", m % 2))))
                ev_at.append(vop(lambda e, mm_=m: e.scalar_tensor_tensor(
                    out=e_at[:, mm_ * TC:(mm_ + 1) * TC], in0=ps[mm_][:, 0:TC],
                    scalar=projb[:, mm_:mm_ + 1],
                    in1=e_xn1[:, (mm_ % 2) * TC:(mm_ % 2) * TC + TC],
                    op0=ALU.add, op1=ALU.add),
                    deps=(ev_m, evn, gev("e_at_free"), evc)))
                sev(("e_xn1", m % 2), ev_at[m])
                bank_used(m, ev_at[m])
            ers2, enm2 = stats(e_at, TC, deps=tuple(ev_at))
            rs3 = bcb[:, 0:TC].unsqueeze(1).broadcast_to([128, NKD, TC])
            nm3 = bcb[:, TC:2 * TC].unsqueeze(1).broadcast_to([128, NKD, TC])
            at3 = e_at.ap().rearrange("p (m t) -> p m t", m=NKD)
            x13 = e_x1.ap().rearrange("p (m t) -> p m t", m=NKD)
            x33 = e_x3.ap().rearrange("p (m t) -> p m t", m=NKD)
            e1 = vop(lambda e: e.tensor_tensor(out=x13, in0=at3, in1=rs3,
                                               op=ALU.mult),
                     deps=(ers2, gev("e_x1_free")) + tuple(ev_at))
            e2 = vop(lambda e: e.tensor_tensor(out=x13, in0=x13, in1=nm3,
                                               op=ALU.add), deps=(enm2, e1))
            e3 = vop(lambda e: e.tensor_tensor(out=e_x1[:], in0=e_x1[:],
                                               in1=p1x[:], op=ALU.add),
                     deps=(e2, ev_x0))
            ev_x1 = [e3] * NKD
            sev("e_at_free", e3)
            sev("p1x_free", e3)
            sev("bcb_free", e3)
            ers3, enm3 = stats(e_x1, TC, deps=(e3,), scl=QSCL)
            e1 = vop(lambda e: e.tensor_tensor(out=x33, in0=x13, in1=rs3,
                                               op=ALU.mult),
                     deps=(ers3, e3, gev("e_x3_free")))
            e2 = vop(lambda e: e.tensor_tensor(out=x33, in0=x33, in1=nm3,
                                               op=ALU.add), deps=(enm3, e1))
            ev_x3 = [e2] * NKD
            sev("bcb_free", e2)

            all_dn = []
            ev_hp = {}
            ev_g = None
            for m in range(NKH):
                ws = m % 2
                srcg = wgT.rearrange("(kt p) h -> p kt h", p=128)[
                    :, :, m * 128:(m + 1) * 128]
                srcu = wuT.rearrange("(kt p) h -> p kt h", p=128)[
                    :, :, m * 128:(m + 1) * 128]
                ev_wg_ = em.dma("gpsimd",
                                e_wg[:, ws * 1024:(ws + 1) * 1024]
                                .rearrange("p (kt c) -> p kt c", c=128),
                                srcg, f"ewg{ws}", deps=(gev(("e_wg", ws)),))
                ev_wu_ = em.dma("gpsimd",
                                e_wu[:, ws * 1024:(ws + 1) * 1024]
                                .rearrange("p (kt c) -> p kt c", c=128),
                                srcu, f"ewu{ws}", deps=(gev(("e_wu", ws)),))
                pgb = 0 + (m % 2)
                pub = 2 + (m % 2)
                pg = ps[pgb][:, 0:TC]
                pu = ps[pub][:, 0:TC]
                lg = [(pg, e_wg[:, ws * 1024 + kt * 128:ws * 1024 + (kt + 1) * 128],
                       e_x3[:, kt * TC:(kt + 1) * TC]) for kt in range(NKD)]
                lu = [(pu, e_wu[:, ws * 1024 + kt * 128:ws * 1024 + (kt + 1) * 128],
                       e_x3[:, kt * TC:(kt + 1) * TC]) for kt in range(NKD)]
                ev_g = mmg(lg, [pgb], deps=(ev_wg_, ev_x3[-1]))
                ev_u_ = mmg(lu, [pub], deps=(ev_wu_,))
                sev(("e_wg", ws), ev_g)
                sev(("e_wu", ws), ev_u_)
                gsl = e_gt[:, (m % 2) * TC:(m % 2) * TC + TC]
                ev_si = sop(lambda e, o=gsl, i=pg: e.activation(out=o, in_=i,
                                                               func=ACT.Silu),
                            deps=(ev_g, gev(("e_gt", m % 2))))
                bank_used(pgb, ev_si)
                ev_h = vop(lambda e, mm_=m, g=gsl, u=pu: e.scalar_tensor_tensor(
                    out=e_hp[:, mm_ * TC:(mm_ + 1) * TC], in0=g, scalar=QSCL,
                    in1=u, op0=ALU.mult, op1=ALU.mult),
                    deps=(ev_si, ev_u_, gev(("e_hp", m))))
                bank_used(pub, ev_h)
                sev(("e_gt", m % 2), ev_h)
                ev_hp[m] = ev_h
            ev_dn = {}
            for sw in range(2):
                ev = None
                for m in range(NKH):
                    ws = m % 2
                    src = wdT.rearrange("(m p) o -> m p o", p=128)[
                        m, :, sw * 512:(sw + 1) * 512]
                    ev_wd_ = em.dma("gpsimd", e_wd[:, ws * 512:(ws + 1) * 512],
                                    src, f"ewd{ws}", deps=(gev(("e_wd", ws)),))
                    for m2 in range(4):
                        g = sw * 4 + m2
                        bk = 4 + m2
                        dps = [ev_wd_, ev_hp[m]]
                        if m == 0:
                            dps.append(bank_free[bk])
                        ev = em.op("tensor",
                                   (lambda e, b=bk, mm_=m, ws_=ws, c2=m2,
                                    s0=(m == 0), s1=(m == NKH - 1):
                                    e.matmul(ps[b][:, 0:TC],
                                             e_wd[:, ws_ * 512 + c2 * 128:
                                                  ws_ * 512 + c2 * 128 + 128],
                                             e_hp[:, mm_ * TC:(mm_ + 1) * TC],
                                             start=s0, stop=s1)),
                                   inc=("tv" if (m == NKH - 1 or m2 == 3) else None),
                                   deps=tuple(dps))
                        if m == NKH - 1:
                            ev_dn[g] = ev
                    sev(("e_wd", ws), ev)
                for m2 in range(4):
                    g = sw * 4 + m2
                    ev_e = sop(lambda e, gg=g, b=4 + m2: e.copy(
                        out=e_at[:, gg * TC:(gg + 1) * TC],
                        in_=ps[b][:, 0:TC]),
                        deps=(ev_dn[g], gev("e_ffn_free")))
                    bank_used(4 + m2, ev_e)
                    ev_dn[g] = ev_e
                    all_dn.append(ev_e)
            for m in range(NKH):
                sev(("e_hp", m), ev_dn[NKD - 1])
            ers4, enm4 = stats(e_at, TC, deps=tuple(all_dn))
            e1 = vop(lambda e: e.tensor_tensor(out=x33, in0=at3, in1=rs3,
                                               op=ALU.mult),
                     deps=(ers4, gev("x3_out_free"), ev_g))
            e2 = vop(lambda e: e.tensor_tensor(out=x33, in0=x33, in1=nm3,
                                               op=ALU.add), deps=(enm4, e1))
            e3 = vop(lambda e: e.tensor_tensor(out=e_x3[:], in0=e_x3[:],
                                               in1=e_x1[:], op=ALU.add),
                     deps=(e2, ev_x1[-1]))
            ev_o = [e3] * NKD
            sev("e_ffn_free", ev_o[-1])
            sev("e_x1_free", ev_o[-1])
            sev("e_at_free", ev_o[-1])
            sev("bcb_free", ev_o[-1])
            ev_out = em.dma("gpsimd", yTv[:, :, t0:t0 + TC],
                            e_x3.ap().rearrange("p (kt t) -> p kt t", t=TC), "do",
                            deps=(ev_o[-1],))
            sev("x3_out_free", ev_out)
            sev("e_x3_free", ev_out)

        for j in range(NCH):
            in_proj_chunk("f", True, j)
            xp_dtp_chunk("f", j)
            if j > 0:
                scan_chunk("f", True, j - 1)
        scan_chunk("f", True, NCH - 1)
        for j in range(NCH - 1, -1, -1):
            if j + 2 < NCH:
                p89_epi(j + 2)
            in_proj_chunk("b", False, j)
            xp_dtp_chunk("b", j)
            if j + 1 < NCH:
                scan_chunk("b", False, j + 1)
                p89_mm(j + 1)
        scan_chunk("b", False, 0)
        p89_epi(1)
        p89_mm(0)
        p89_epi(0)

        em.op("gpsimd", lambda e: e.engine_nop(),
              deps=(("do", em.cnt.get("do", 0)),))

        sems = {}
        for name in em.cnt:
            sems[name] = stack.enter_context(nc.semaphore(name))
        blk = stack.enter_context(nc.Block())
        em.replay(blk, sems)
    lower_extended_insts(nc)
    return nc


def _col_form(v, nt):
    return np.ascontiguousarray(np.asarray(v, np.float32).reshape(nt, 128).T)


def _prep_core_inputs(xb, ins):
    m = {}
    m["xT"] = np.ascontiguousarray(np.asarray(xb, np.float32).T)
    for d, sfx in (("f", "_f"), ("b", "_b")):
        m[f"inwT_{d}"] = np.ascontiguousarray(np.asarray(ins[f"L0_attn_in_w{sfx}"], np.float32).T)
        m[f"outw_{d}"] = np.ascontiguousarray(np.asarray(ins[f"L0_attn_out_w{sfx}"], np.float32))
        cwv = np.asarray(ins[f"L0_attn_conv_w{sfx}"], np.float32).reshape(DI, 4)
        m[f"cw_{d}"] = np.ascontiguousarray(
            cwv.reshape(NCT, 128, 4).transpose(1, 0, 2).reshape(128, NCT * 4))
        m[f"cb_{d}"] = _col_form(ins[f"L0_attn_conv_b{sfx}"], NCT)
        xpv = np.asarray(ins[f"L0_attn_xp_w{sfx}"], np.float32).T
        m[f"xpw_{d}"] = np.ascontiguousarray(
            xpv.reshape(NCT, 128, 96).transpose(1, 0, 2).reshape(128, NCT * 96))
        m[f"dtpw_{d}"] = np.ascontiguousarray(np.asarray(ins[f"L0_attn_dtp_w{sfx}"], np.float32).T)
        m[f"dtpb_{d}"] = _col_form(ins[f"L0_attn_dtp_b{sfx}"], NCT)
        av = np.asarray(ins[f"L0_attn_A_log{sfx}"], np.float32)
        m[f"A_{d}"] = np.ascontiguousarray(
            av.reshape(NCT, 128, NS).transpose(1, 0, 2).reshape(128, NCT * NS))
        m[f"Dsk_{d}"] = _col_form(ins[f"L0_attn_D{sfx}"], NCT)
    m["gamR"] = _col_form(ins["L0_attn_ln_gamma"], NKD)
    m["betR"] = _col_form(ins["L0_attn_ln_beta"], NKD)
    m["projT"] = np.ascontiguousarray(np.asarray(ins["L0_attn_proj_w"], np.float32).T)
    m["projbR"] = _col_form(ins["L0_attn_proj_b"], NKD)
    m["wgT"] = np.ascontiguousarray(np.asarray(ins["L0_ffn_w_gate"], np.int32).T)
    m["wuT"] = np.ascontiguousarray(np.asarray(ins["L0_ffn_w_up"], np.int32).T)
    m["wdT"] = np.ascontiguousarray(np.asarray(ins["L0_ffn_w_down"], np.int32).T)
    return m


def kernel(**inputs):
    x = np.asarray(inputs["L0_attn_x"])
    B, L, _ = x.shape
    nc = build_nc(L=L)
    in_maps = [_prep_core_inputs(x[b], inputs) for b in range(B)]
    res = run_bass_kernel_spmd(nc, in_maps, core_ids=list(range(B)))
    out = np.stack([np.asarray(r["yT"]).T for r in res.results], axis=0)
    return np.ascontiguousarray(out.astype(np.float32))
